# version 01
# baseline (speedup 1.0000x reference)
import numpy as np
import ml_dtypes
from contextlib import ExitStack
import concourse.bass as bass
import concourse.mybir as mybir
from concourse.bass_utils import run_bass_kernel_spmd

F32 = mybir.dt.float32
BF16 = mybir.dt.bfloat16
AF = mybir.ActivationFunctionType
ALU = mybir.AluOpType
AX = mybir.AxisListType
BF = ml_dtypes.bfloat16

NCORE = 8
BSZ, SEQ, D = 4, 8192, 1024
TT = BSZ * SEQ
TPC = TT // NCORE


class P:

    def __init__(self, nc, block, ld, cs, st):
        self.nc, self.block = nc, block
        self.ld, self.cs, self.st = ld, cs, st
        self.l = 0
        self.c = 0
        self.s = 0

    def on(self, engname, f):
        getattr(self.block, engname)(f)

    def t(self, instr):
        instr.then_inc(self.cs, 1)
        self.c += 1

    def dma(self, q, out, in_):
        q.dma_start(out=out, in_=in_).then_inc(self.ld, 16)
        self.l += 16

    def sto(self, q, out, in_):
        q.dma_start(out=out, in_=in_).then_inc(self.st, 16)
        self.s += 16

    def gate(self, eng):
        eng.wait_ge(self.cs, self.c)
        if self.l:
            eng.wait_ge(self.ld, self.l)

    def gatec(self, eng):
        eng.wait_ge(self.cs, self.c)

    def gates(self, eng):
        if self.s:
            eng.wait_ge(self.st, self.s)


def mkprog(name):
    nc = bass.Bass("TRN2", target_bir_lowering=False)
    stack = ExitStack()
    ld = stack.enter_context(nc.semaphore("ld"))
    cs = stack.enter_context(nc.semaphore("cs"))
    st = stack.enter_context(nc.semaphore("st"))
    return nc, stack, ld, cs, st


def ln_tile(pr, x_in, out_bf, sc, r=None):
    v, s = 'vector', 'scalar'
    cols = sc['cols'][:]
    sq = sc['sq'][:]

    def f1(e):
        pr.gate(e)
        pr.t(e.reduce_sum(cols[:, 0:1], x_in, AX.X))
    pr.on(v, f1)

    def f2(e):
        pr.gatec(e)
        pr.t(e.activation(sq, x_in, AF.Square))
    pr.on(s, f2)

    def f3(e):
        pr.gatec(e)
        pr.t(e.reduce_sum(cols[:, 1:2], sq, AX.X))
        pr.gatec(e)
        pr.t(e.tensor_scalar_mul(cols[:, 0:1], cols[:, 0:1], 1.0 / D))
        pr.gatec(e)
        pr.t(e.tensor_scalar_mul(cols[:, 1:2], cols[:, 1:2], 1.0 / D))
        pr.gatec(e)
        pr.t(e.tensor_mul(cols[:, 2:3], cols[:, 0:1], cols[:, 0:1]))
        pr.gatec(e)
        pr.t(e.tensor_sub(cols[:, 3:4], cols[:, 1:2], cols[:, 2:3]))
    pr.on(v, f3)

    def f4(e):
        pr.gatec(e)
        pr.t(e.activation(cols[:, 4:5], cols[:, 3:4], AF.Ln, bias=sc['eps']))
        pr.gatec(e)
        pr.t(e.activation(cols[:, 4:5], cols[:, 4:5], AF.Exp, scale=-0.5))
    pr.on(s, f4)

    def f5(e):
        pr.gatec(e)
        pr.t(e.scalar_tensor_tensor(cols[:, 5:6], cols[:, 0:1], -1.0, cols[:, 4:5],
                                    ALU.mult, ALU.mult))
        pr.gatec(e)
        pr.t(e.tensor_scalar(out_bf, x_in, cols[:, 4:5], cols[:, 5:6], ALU.mult, ALU.add))
        if r is not None:
            pr.gatec(e)
            pr.t(e.tensor_scalar(r, x_in, cols[:, 4:5], cols[:, 5:6], ALU.mult, ALU.add))
    pr.on(v, f5)


def mkconsts(pr, stack, nc, vals):
    kt = stack.enter_context(nc.sbuf_tensor("kconst", [128, max(1, len(vals))], F32))
    aps = {}
    def f(e):
        for i, v in enumerate(vals):
            pr.t(e.memset(kt[:, i:i + 1], float(v)))
    pr.on('vector', f)
    for i, v in enumerate(vals):
        aps[v] = kt[:, i:i + 1]
    return aps



def build_wcast(K):
    nc, stack, ld, cs, st = mkprog("wcast")
    win = nc.dram_tensor("win", [128, K], F32, kind="ExternalInput").ap()
    wout = nc.dram_tensor("wout", [128, K], BF16, kind="ExternalOutput").ap()
    CH = 4096
    nch = (K + CH - 1) // CH
    with nc.Block() as block:
        pr = P(nc, block, ld, cs, st)
        sb = stack.enter_context(nc.sbuf_tensor("sb", [128, 2 * CH], F32))
        ob = stack.enter_context(nc.sbuf_tensor("ob", [128, 2 * CH], BF16))
        for i in range(nch):
            c0, c1 = i * CH, min(K, (i + 1) * CH)
            w = c1 - c0
            buf = (i % 2) * CH

            def fl(e, c0=c0, c1=c1, w=w, buf=buf, i=i):
                if i >= 2:
                    e.wait_ge(st, 16 * (i - 1))
                pr.dma(e, sb[:, buf:buf + w], win[:, c0:c1])
            pr.on('sync', fl)

            def fc(e, w=w, buf=buf):
                pr.gate(e)
                pr.t(e.copy(ob[:, buf:buf + w], sb[:, buf:buf + w]))
            pr.on('scalar', fc)

            def fs(e, c0=c0, c1=c1, w=w, buf=buf):
                pr.gatec(e)
                pr.sto(e, wout[:, c0:c1], ob[:, buf:buf + w])
            pr.on('gpsimd', fs)

        def fin(e):
            e.wait_ge(st, pr.s)
        pr.on('gpsimd', fin)
    stack.close()
    return nc


def wcast_all(weights):
    names = sorted(weights.keys())
    flats = [np.ascontiguousarray(weights[n]).reshape(-1) for n in names]
    sizes = [f.size for f in flats]
    big = np.concatenate(flats)
    per = 8 * 128
    K = (big.size + per - 1) // per
    big = np.pad(big, (0, per * K - big.size))
    big = big.reshape(8, 128, K)
    nc = build_wcast(K)
    in_maps = [{"win": np.ascontiguousarray(big[i])} for i in range(8)]
    res = run_bass_kernel_spmd(nc, in_maps, core_ids=list(range(8)))
    out = np.concatenate([r["wout"].reshape(-1) for r in res.results])
    o = {}
    off = 0
    for n, sz in zip(names, sizes):
        o[n] = out[off:off + sz].reshape(weights[n].shape)
        off += sz
    return o



def build_tffn():
    nc, stack, ld, cs, st = mkprog("tffn")
    xin = nc.dram_tensor("x", [TPC, D], F32, kind="ExternalInput").ap()
    w1 = nc.dram_tensor("w1", [1024, 4096], BF16, kind="ExternalInput").ap()
    w2 = nc.dram_tensor("w2", [1024, 4096], BF16, kind="ExternalInput").ap()
    h1 = nc.dram_tensor("h1", [4096, 4096], BF16, kind="ExternalInput").ap()
    h2 = nc.dram_tensor("h2", [4096, 4096], BF16, kind="ExternalInput").ap()
    w3 = nc.dram_tensor("w3", [4096, 1024], BF16, kind="ExternalInput").ap()
    s1 = nc.dram_tensor("s1", [128, 32], F32, kind="ExternalInput").ap()
    s2 = nc.dram_tensor("s2", [128, 32], F32, kind="ExternalInput").ap()
    sh1 = nc.dram_tensor("sh1", [128, 32], F32, kind="ExternalInput").ap()
    sh2 = nc.dram_tensor("sh2", [128, 32], F32, kind="ExternalInput").ap()
    s3 = nc.dram_tensor("s3", [128, 8], F32, kind="ExternalInput").ap()
    idb = nc.dram_tensor("c_idb", [128, 128], BF16, kind="ExternalInput").ap()
    idf = nc.dram_tensor("c_idf", [128, 128], F32, kind="ExternalInput").ap()
    out = nc.dram_tensor("out", [TPC, D], F32, kind="ExternalOutput").ap()

    TC = 512
    NTC = TPC // TC

    sb = lambda n, shp, dt: stack.enter_context(nc.sbuf_tensor(n, shp, dt))
    ps = lambda n, shp: stack.enter_context(nc.psum_tensor(n, shp, F32))

    with nc.Block() as block:
        pr = P(nc, block, ld, cs, st)
        kq = mkconsts(pr, stack, nc, [1e-5])
        xsb = sb("xsb", [128, (TC // 128) * 1024], F32)
        xlt = sb("xlt", [128, 1024], BF16)
        sq = sb("sq", [128, 1024], F32)
        cols = sb("cols", [128, 8], F32)
        sid = sb("sid", [128, 128], BF16)
        sidf = sb("sidf", [128, 128], F32)
        xf = sb("xf", [128, 8 * TC], BF16)
        Ha = sb("Ha", [128, 32 * TC], BF16)
        Hb = sb("Hb", [128, 32 * TC], BF16)
        wsb = sb("wsb", [128, 2 * 8 * 128], BF16)
        hw = sb("hw", [128, 32 * 128], BF16)
        w3b = sb("w3b", [128, 32 * 128], BF16)
        otok = sb("otok", [128, 1024], F32)
        tmpG = sb("tmpG", [128, 512], F32)
        ttile = sb("ttile", [128, 128], F32)
        pA = ps("pA", [128, 512])
        pB = ps("pB", [128, 512])
        pT = stack.enter_context(nc.psum_tensor("pT", [128, 128], BF16))
        pTf = ps("pTf", [128, 128])

        scl = sb("scl", [128, 4 * 32 + 8], F32)

        def f0b(e):
            pr.dma(e, sid[:], idb)
            pr.dma(e, sidf[:], idf)
            pr.dma(e, scl[:, 0:32], s1)
            pr.dma(e, scl[:, 32:64], s2)
            pr.dma(e, scl[:, 64:96], sh1)
            pr.dma(e, scl[:, 96:128], sh2)
            pr.dma(e, scl[:, 128:136], s3)
        pr.on('sync', f0b)

        def wslice(w, jo, nki):
            return w[:, jo * 128:(jo + 1) * 128].rearrange("(kc p) j -> p kc j", p=128)

        def s3d(t, c0, c1):
            return t[:, c0:c1].rearrange("p (kc j) -> p kc j", j=128)

        for tc in range(NTC):
            for r in range(TC // 128):
                def fl(e, tc=tc, r=r):
                    pr.gatec(e)
                    pr.dma(e, xsb[:, r * 1024:(r + 1) * 1024],
                           xin[tc * TC + r * 128: tc * TC + (r + 1) * 128, :])
                pr.on('sync', fl)
                ln_tile(pr, xsb[:, r * 1024:(r + 1) * 1024], xlt[:], {'sq': sq, 'cols': cols, 'eps': kq[1e-5]})
                for kc in range(8):
                    def ftr(e, kc=kc):
                        pr.gate(e)
                        pr.t(e.transpose(pT[:, 0:128], xlt[:, kc * 128:(kc + 1) * 128], sid[:]))
                    pr.on('tensor', ftr)

                    def fcp(e, kc=kc, r=r):
                        pr.gatec(e)
                        pr.t(e.copy(xf[:, kc * TC + r * 128: kc * TC + (r + 1) * 128], pT[:, 0:128]))
                    pr.on('scalar', fcp)

            for jo in range(32):
                def fw(e, jo=jo):
                    pr.gatec(e)
                    pr.dma(e, s3d(wsb, 0, 1024), wslice(w1, jo, 8))
                    pr.dma(e, s3d(wsb, 1024, 2048), wslice(w2, jo, 8))
                pr.on('sync', fw)
                for tg in range(TC // 512):
                    def fmm(e, jo=jo, tg=tg):
                        pr.gate(e)
                        for ki in range(8):
                            m = e.matmul(pA[:, 0:512], wsb[:, ki * 128:(ki + 1) * 128],
                                         xf[:, ki * TC + tg * 512: ki * TC + tg * 512 + 512],
                                         start=(ki == 0), stop=(ki == 7))
                            if ki == 7:
                                pr.t(m)
                        for ki in range(8):
                            m = e.matmul(pB[:, 0:512], wsb[:, 1024 + ki * 128: 1024 + (ki + 1) * 128],
                                         xf[:, ki * TC + tg * 512: ki * TC + tg * 512 + 512],
                                         start=(ki == 0), stop=(ki == 7))
                            if ki == 7:
                                pr.t(m)
                    pr.on('tensor', fmm)

                    def fev(e, jo=jo, tg=tg):
                        pr.gatec(e)
                        pr.t(e.activation(tmpG[:], pA[:, 0:512], AF.Silu, scale=scl[:, jo:jo + 1]))
                    pr.on('scalar', fev)

                    def fev2(e, jo=jo, tg=tg):
                        pr.gatec(e)
                        pr.t(e.scalar_tensor_tensor(Ha[:, jo * TC + tg * 512: jo * TC + tg * 512 + 512],
                                                    pB[:, 0:512], scl[:, 32 + jo:33 + jo], tmpG[:],
                                                    ALU.mult, ALU.mult))
                    pr.on('vector', fev2)

            for (hsrc, hdst, wT, scoff) in ((Ha, Hb, h1, 64), (Hb, Ha, h2, 96)):
                for jo in range(32):
                    def fw(e, jo=jo, wT=wT):
                        pr.gatec(e)
                        for q in range(4):
                            pr.dma(e, s3d(hw, q * 1024, (q + 1) * 1024),
                                   wT[q * 1024:(q + 1) * 1024, jo * 128:(jo + 1) * 128]
                                   .rearrange("(kc p) j -> p kc j", p=128))
                    pr.on('sync', fw)
                    for tg in range(TC // 512):
                        def fmm(e, jo=jo, tg=tg, hsrc=hsrc):
                            pr.gate(e)
                            for ki in range(32):
                                m = e.matmul(pA[:, 0:512], hw[:, ki * 128:(ki + 1) * 128],
                                             hsrc[:, ki * TC + tg * 512: ki * TC + tg * 512 + 512],
                                             start=(ki == 0), stop=(ki == 31))
                                if ki == 31:
                                    pr.t(m)
                        pr.on('tensor', fmm)

                        def fev(e, jo=jo, tg=tg, hdst=hdst, scoff=scoff):
                            pr.gatec(e)
                            pr.t(e.activation(hdst[:, jo * TC + tg * 512: jo * TC + tg * 512 + 512],
                                              pA[:, 0:512], AF.Gelu, scale=scl[:, scoff + jo:scoff + jo + 1]))
                        pr.on('scalar', fev)

            for do in range(8):
                def fw(e, do=do):
                    pr.gatec(e)
                    for q in range(4):
                        pr.dma(e, s3d(w3b, q * 1024, (q + 1) * 1024),
                               w3[q * 1024:(q + 1) * 1024, do * 128:(do + 1) * 128]
                               .rearrange("(kc p) j -> p kc j", p=128))
                pr.on('sync', fw)
                for tg in range(TC // 512):
                    def fmm(e, do=do, tg=tg):
                        pr.gate(e)
                        for ki in range(32):
                            m = e.matmul(pA[:, 0:512], w3b[:, ki * 128:(ki + 1) * 128],
                                         Ha[:, ki * TC + tg * 512: ki * TC + tg * 512 + 512],
                                         start=(ki == 0), stop=(ki == 31))
                            if ki == 31:
                                pr.t(m)
                    pr.on('tensor', fmm)

                    def fsc(e, do=do, tg=tg):
                        pr.gatec(e)
                        pr.t(e.tensor_scalar_mul(Hb[:, 0:512] if False else tmpG[:],
                                                 pA[:, 0:512], scl[:, 128 + do:129 + do]))
                    pr.on('vector', fsc)
                    for q in range(4):
                        def ftr(e, q=q):
                            pr.gate(e)
                            pr.t(e.transpose(pTf[:, 0:128], tmpG[:, q * 128:(q + 1) * 128], sidf[:]))
                        pr.on('tensor', ftr)

                        def fadd(e, do=do, tg=tg, q=q):
                            pr.gatec(e)
                            pr.gates(e)
                            r = tg * 4 + q
                            pr.t(e.scalar_tensor_tensor(otok[:, do * 128:(do + 1) * 128],
                                                        pTf[:, 0:128], 0.5,
                                                        xsb[:, r * 1024 + do * 128: r * 1024 + (do + 1) * 128],
                                                        ALU.mult, ALU.add))
                        pr.on('vector', fadd)

                        def fst(e, do=do, tg=tg, q=q, tc=tc):
                            pr.gatec(e)
                            r = tg * 4 + q
                            pr.sto(e, out[tc * TC + r * 128: tc * TC + (r + 1) * 128,
                                          do * 128:(do + 1) * 128],
                                   otok[:, do * 128:(do + 1) * 128])
                        pr.on('gpsimd', fst)

        def fin(e):
            e.wait_ge(st, pr.s)
        pr.on('gpsimd', fin)
    stack.close()
    return nc


_tffn_nc = None


def run_tffn(x_shards, wb, scales):
    global _tffn_nc
    if _tffn_nc is None:
        _tffn_nc = build_tffn()
    idb = np.eye(128, dtype=BF)
    idf = np.eye(128, dtype=np.float32)
    im = []
    for c in range(8):
        im.append({"x": x_shards[c], "w1": wb['l1_w1T'], "w2": wb['l1_w2T'],
                   "h1": wb['l1_h1T'], "h2": wb['l1_h2T'], "w3": wb['l1_w3T'],
                   "s1": scales['w1_s'], "s2": scales['w2_s'], "sh1": scales['h1_s'],
                   "sh2": scales['h2_s'], "s3": scales['w3_s'],
                   "c_idb": idb, "c_idf": idf})
    res = run_bass_kernel_spmd(_tffn_nc, im, core_ids=list(range(8)))
    return [r["out"] for r in res.results]



def build_moe():
    nc, stack, ld, cs, st = mkprog("moe")
    xin = nc.dram_tensor("x", [TPC, D], F32, kind="ExternalInput").ap()
    wgt = nc.dram_tensor("wgt", [1024, 8], BF16, kind="ExternalInput").ap()
    bg = nc.dram_tensor("bg", [1, 8], F32, kind="ExternalInput").ap()
    wgate = nc.dram_tensor("wgate", [8, 1024, 4096], BF16, kind="ExternalInput").ap()
    wup = nc.dram_tensor("wup", [8, 1024, 4096], BF16, kind="ExternalInput").ap()
    wdown = nc.dram_tensor("wdown", [8, 4096, 1024], BF16, kind="ExternalInput").ap()
    idb = nc.dram_tensor("c_idb", [128, 128], BF16, kind="ExternalInput").ap()
    idf = nc.dram_tensor("c_idf", [128, 128], F32, kind="ExternalInput").ap()
    e8 = nc.dram_tensor("c_e8", [8, 8 * 128], F32, kind="ExternalInput").ap()
    onesf = nc.dram_tensor("c_ones", [1, 128], F32, kind="ExternalInput").ap()
    out = nc.dram_tensor("out", [TPC, D], F32, kind="ExternalOutput").ap()

    TH = 2048
    sbt = lambda n, shp, dt: stack.enter_context(nc.sbuf_tensor(n, shp, dt))
    ps = lambda n, shp: stack.enter_context(nc.psum_tensor(n, shp, F32))

    with nc.Block() as block:
        pr = P(nc, block, ld, cs, st)
        kq = mkconsts(pr, stack, nc, [1e-5])
        xt = sbt("xt", [128, 1024], F32)
        xlt = sbt("xlt", [128, 1024], BF16)
        sq = sbt("sq", [128, 1024], F32)
        cols = sbt("cols", [128, 16], F32)
        sid = sbt("sid", [128, 128], BF16)
        sidf = sbt("sidf", [128, 128], F32)
        sones = sbt("sones", [1, 128], F32)
        se8 = sbt("se8", [8, 8 * 128], F32)
        wgtb = sbt("wgtb", [128, 64], BF16)
        bgb = sbt("bgb", [128, 8], F32)
        xf = sbt("xf", [128, 8 * TH], BF16)
        wtsall = sbt("wtsall", [128, 16 * 8], F32)
        wrow = sbt("wrow", [8, TH], F32)
        wbc = sbt("wbc", [128, TH], F32)
        H = sbt("H", [128, 4 * TH], BF16)
        oacc = sbt("oacc", [128, 8 * TH], F32)
        wgs = sbt("wgs", [128, 8 * 512], BF16)
        wus = sbt("wus", [128, 8 * 512], BF16)
        wds = sbt("wds", [128, 4 * 1024], BF16)
        tmpG = sbt("tmpG", [128, 512], F32)
        tmpU = sbt("tmpU", [128, 512], F32)
        otok = sbt("otok", [128, 1024], F32)
        pG = ps("pG", [128, 512])
        pU = ps("pU", [128, 512])
        pD = ps("pD", [128, 512])
        pW = ps("pW", [128, 512])
        pL = ps("pL", [128, 8])
        pT = stack.enter_context(nc.psum_tensor("pT", [128, 128], BF16))
        pTf = ps("pTf", [128, 128])

        def f0(e):
            pr.dma(e, sid[:], idb)
            pr.dma(e, sidf[:], idf)
            pr.dma(e, sones[:], onesf)
            pr.dma(e, se8[:], e8)
            pr.dma(e, wgtb[:], wgt.rearrange("(kc p) j -> p kc j", p=128))
        pr.on('sync', f0)

        def f1(e):
            pr.gate(e)
            pr.t(e.matmul(pL[:, 0:8], sones[:], bg, start=True, stop=True))
        bgs = sbt("bgs", [1, 8], F32)

        def f0b(e):
            pr.dma(e, bgs[:], bg)
        pr.on('sync', f0b)

        def f1b(e):
            pr.gate(e)
            pr.t(e.matmul(pL[:, 0:8], sones[:], bgs[:], start=True, stop=True))
        pr.on('tensor', f1b)

        def f1c(e):
            pr.gatec(e)
            pr.t(e.copy(bgb[:], pL[:, 0:8]))
        pr.on('scalar', f1c)

        for half in range(2):
            t0 = half * TH
            for r in range(16):
                def fl(e, r=r, t0=t0):
                    pr.gatec(e)
                    pr.dma(e, xt[:], xin[t0 + r * 128: t0 + (r + 1) * 128, :])
                pr.on('sync', fl)
                ln_tile(pr, xt[:], xlt[:], {'sq': sq, 'cols': cols, 'eps': kq[1e-5]})
                for kc2 in range(8):
                    def ftr(e, kc2=kc2):
                        pr.gate(e)
                        pr.t(e.transpose(pT[:, 0:128], xlt[:, kc2 * 128:(kc2 + 1) * 128], sid[:]))
                    pr.on('tensor', ftr)

                    def fcp(e, kc2=kc2, r=r):
                        pr.gatec(e)
                        pr.t(e.copy(xf[:, kc2 * TH + r * 128: kc2 * TH + (r + 1) * 128], pT[:, 0:128]))
                    pr.on('scalar', fcp)

                def flg(e, r=r):
                    pr.gatec(e)
                    for ki in range(8):
                        m = e.matmul(pL[:, 0:8], xf[:, ki * TH + r * 128: ki * TH + (r + 1) * 128],
                                     wgtb[:, ki * 8:(ki + 1) * 8], start=(ki == 0), stop=(ki == 7))
                        if ki == 7:
                            pr.t(m)
                pr.on('tensor', flg)

                def ftp(e, r=r):
                    L = wtsall[:, r * 8:(r + 1) * 8]
                    pr.gatec(e)
                    pr.t(e.tensor_add(L, pL[:, 0:8], bgb[:]))
                    pr.gatec(e)
                    pr.t(e.reduce_max(cols[:, 8:9], L, AX.X))
                    pr.gatec(e)
                    pr.t(e.tensor_scalar(sq[:, 0:8], L, cols[:, 8:9], None, ALU.is_ge))
                    pr.gatec(e)
                    pr.t(e.scalar_tensor_tensor(sq[:, 8:16], sq[:, 0:8], -1e9, L, ALU.mult, ALU.add))
                    pr.gatec(e)
                    pr.t(e.reduce_max(cols[:, 9:10], sq[:, 8:16], AX.X))
                    pr.gatec(e)
                    pr.t(e.tensor_sub(cols[:, 10:11], cols[:, 9:10], cols[:, 8:9]))
                pr.on('vector', ftp)

                def ftp2(e, r=r):
                    pr.gatec(e)
                    pr.t(e.activation(cols[:, 11:12], cols[:, 10:11], AF.Exp))
                pr.on('scalar', ftp2)

                def ftp3(e, r=r):
                    L = wtsall[:, r * 8:(r + 1) * 8]
                    pr.gatec(e)
                    pr.t(e.tensor_scalar(sq[:, 16:24], L, cols[:, 9:10], None, ALU.is_ge))
                    pr.gatec(e)
                    pr.t(e.tensor_scalar(sq[:, 24:32], L, cols[:, 8:9], None, ALU.subtract))
                pr.on('vector', ftp3)

                def ftp4(e, r=r):
                    pr.gatec(e)
                    pr.t(e.activation(sq[:, 32:40], sq[:, 24:32], AF.Exp))
                pr.on('scalar', ftp4)

                def ftp5(e, r=r):
                    L = wtsall[:, r * 8:(r + 1) * 8]
                    pr.gatec(e)
                    pr.t(e.tensor_scalar_add(cols[:, 12:13], cols[:, 11:12], 1.0))
                    pr.gatec(e)
                    pr.t(e.reciprocal(cols[:, 13:14], cols[:, 12:13]))
                    pr.gatec(e)
                    pr.t(e.tensor_mul(sq[:, 40:48], sq[:, 16:24], sq[:, 32:40]))
                    pr.gatec(e)
                    pr.t(e.tensor_scalar(L, sq[:, 40:48], cols[:, 13:14], None, ALU.mult))
                pr.on('vector', ftp5)

                def ftw(e, r=r):
                    pr.gate(e)
                    pr.t(e.transpose(pTf[0:8, 0:128], wtsall[:, r * 8:(r + 1) * 8], sidf[:]))
                pr.on('tensor', ftw)

                def ftw2(e, r=r):
                    pr.gatec(e)
                    pr.t(e.copy(wrow[:, r * 128:(r + 1) * 128], pTf[0:8, 0:128]))
                pr.on('scalar', ftw2)

            for ex in range(8):
                for tb in range(4):
                    def fwb(e, ex=ex, tb=tb):
                        pr.gatec(e)
                        pr.t(e.matmul(pW[:, 0:512], se8[:, ex * 128:(ex + 1) * 128],
                                      wrow[:, tb * 512:(tb + 1) * 512], start=True, stop=True))
                    pr.on('tensor', fwb)

                    def fwb2(e, tb=tb):
                        pr.gatec(e)
                        pr.t(e.copy(wbc[:, tb * 512:(tb + 1) * 512], pW[:, 0:512]))
                    pr.on('scalar', fwb2)
                for s in range(8):
                    def fws(e, ex=ex, s=s):
                        pr.gatec(e)
                        pr.dma(e, wgs[:].rearrange("p (kc j) -> p kc j", j=512),
                               wgate[ex][:, s * 512:(s + 1) * 512].rearrange("(kc p) j -> p kc j", p=128))
                        pr.dma(e, wus[:].rearrange("p (kc j) -> p kc j", j=512),
                               wup[ex][:, s * 512:(s + 1) * 512].rearrange("(kc p) j -> p kc j", p=128))
                        pr.dma(e, wds[:].rearrange("p (jc d) -> p jc d", d=1024),
                               wdown[ex][s * 512:(s + 1) * 512, :].rearrange("(jc p) d -> p jc d", p=128))
                    pr.on('sync', fws)
                    for jo in range(4):
                        for tb in range(4):
                            def fmm(e, jo=jo, tb=tb):
                                pr.gate(e)
                                for ki in range(8):
                                    m = e.matmul(pG[:, 0:512], wgs[:, ki * 512 + jo * 128: ki * 512 + (jo + 1) * 128],
                                                 xf[:, ki * TH + tb * 512: ki * TH + (tb + 1) * 512],
                                                 start=(ki == 0), stop=(ki == 7))
                                    if ki == 7:
                                        pr.t(m)
                                for ki in range(8):
                                    m = e.matmul(pU[:, 0:512], wus[:, ki * 512 + jo * 128: ki * 512 + (jo + 1) * 128],
                                                 xf[:, ki * TH + tb * 512: ki * TH + (tb + 1) * 512],
                                                 start=(ki == 0), stop=(ki == 7))
                                    if ki == 7:
                                        pr.t(m)
                            pr.on('tensor', fmm)

                            def fev(e, jo=jo, tb=tb):
                                pr.gatec(e)
                                pr.t(e.activation(tmpG[:], pG[:, 0:512], AF.Silu))
                            pr.on('scalar', fev)

                            def fev2(e, jo=jo, tb=tb):
                                pr.gatec(e)
                                pr.t(e.tensor_mul(tmpU[:], pU[:, 0:512], wbc[:, tb * 512:(tb + 1) * 512]))
                                pr.gatec(e)
                                pr.t(e.tensor_mul(H[:, jo * TH + tb * 512: jo * TH + (tb + 1) * 512],
                                                  tmpG[:], tmpU[:]))
                            pr.on('vector', fev2)
                    for do in range(8):
                        for tb in range(4):
                            def fmm2(e, do=do, tb=tb):
                                pr.gate(e)
                                for jc in range(4):
                                    m = e.matmul(pD[:, 0:512], wds[:, jc * 1024 + do * 128: jc * 1024 + (do + 1) * 128],
                                                 H[:, jc * TH + tb * 512: jc * TH + (tb + 1) * 512],
                                                 start=(jc == 0), stop=(jc == 3))
                                    if jc == 3:
                                        pr.t(m)
                            pr.on('tensor', fmm2)

                            def fac(e, do=do, tb=tb, ex=ex, s=s):
                                pr.gatec(e)
                                dst = oacc[:, do * TH + tb * 512: do * TH + (tb + 1) * 512]
                                if ex == 0 and s == 0:
                                    pr.t(e.tensor_copy(dst, pD[:, 0:512]))
                                else:
                                    pr.t(e.tensor_add(dst, dst, pD[:, 0:512]))
                            pr.on('vector', fac)

            for r in range(16):
                def flr(e, r=r, t0=t0):
                    pr.gatec(e)
                    pr.gates(e)
                    pr.dma(e, xt[:], xin[t0 + r * 128: t0 + (r + 1) * 128, :])
                pr.on('sync', flr)
                for do in range(8):
                    def ftr(e, do=do, r=r):
                        pr.gate(e)
                        pr.t(e.transpose(pTf[:, 0:128], oacc[:, do * TH + r * 128: do * TH + (r + 1) * 128], sidf[:]))
                    pr.on('tensor', ftr)

                    def fadd(e, do=do, r=r):
                        pr.gatec(e)
                        pr.gates(e)
                        pr.t(e.scalar_tensor_tensor(otok[:, do * 128:(do + 1) * 128],
                                                    pTf[:, 0:128], 0.5, xt[:, do * 128:(do + 1) * 128],
                                                    ALU.mult, ALU.add))
                    pr.on('vector', fadd)

                def fst(e, r=r, t0=t0):
                    pr.gatec(e)
                    pr.sto(e, out[t0 + r * 128: t0 + (r + 1) * 128, :], otok[:])
                pr.on('gpsimd', fst)

        def fin(e):
            e.wait_ge(st, pr.s)
        pr.on('gpsimd', fin)
    stack.close()
    return nc


_moe_nc = None


def run_moe(x_shards, wb, bg_v):
    global _moe_nc
    if _moe_nc is None:
        _moe_nc = build_moe()
    idb = np.eye(128, dtype=BF)
    idf = np.eye(128, dtype=np.float32)
    e8 = np.zeros((8, 8 * 128), dtype=np.float32)
    for ex in range(8):
        e8[ex, ex * 128:(ex + 1) * 128] = 1.0
    ones = np.ones((1, 128), dtype=np.float32)
    im = []
    for c in range(8):
        im.append({"x": x_shards[c], "wgt": wb['l0_wgT'], "bg": bg_v.reshape(1, 8),
                   "wgate": wb['l0_wgateT'], "wup": wb['l0_wupT'], "wdown": wb['l0_wdownT'],
                   "c_idb": idb, "c_idf": idf, "c_e8": e8, "c_ones": ones})
    res = run_bass_kernel_spmd(_moe_nc, im, core_ids=list(range(8)))
    return [r["out"] for r in res.results]


TA = 4224


def build_a1():
    nc, stack, ld, cs, st = mkprog("a1")
    xin = nc.dram_tensor("x", [TA, D], F32, kind="ExternalInput").ap()
    win = nc.dram_tensor("win", [1024, 4160], BF16, kind="ExternalInput").ap()
    cw = nc.dram_tensor("cw", [128, 17 * 4], F32, kind="ExternalInput").ap()
    cb = nc.dram_tensor("cb", [128, 17], F32, kind="ExternalInput").ap()
    dtb = nc.dram_tensor("dtb", [1, 32], F32, kind="ExternalInput").ap()
    idb = nc.dram_tensor("c_idb", [128, 128], BF16, kind="ExternalInput").ap()
    onesf = nc.dram_tensor("c_ones", [1, 128], F32, kind="ExternalInput").ap()
    z_out = nc.dram_tensor("z_out", [TA, 2048], F32, kind="ExternalOutput").ap()
    dt_out = nc.dram_tensor("dt_out", [TA, 32], F32, kind="ExternalOutput").ap()
    xh_out = nc.dram_tensor("xh_out", [TA, 2048], BF16, kind="ExternalOutput").ap()
    bct_out = nc.dram_tensor("bct_out", [TA, 32], BF16, kind="ExternalOutput").ap()
    bcf_out = nc.dram_tensor("bcf_out", [32, TA], BF16, kind="ExternalOutput").ap()

    NT = TA // 128
    TGS = [(g * 512, 512) for g in range(TA // 512)] + [(TA - TA % 512, TA % 512)]
    TGS = [(o, w) for (o, w) in TGS if w > 0]

    sbt = lambda n, shp, dt: stack.enter_context(nc.sbuf_tensor(n, shp, dt))
    ps = lambda n, shp: stack.enter_context(nc.psum_tensor(n, shp, F32))
    with nc.Block() as block:
        pr = P(nc, block, ld, cs, st)
        kq = mkconsts(pr, stack, nc, [1e-5])
        xt = sbt("xt", [128, 1024], F32)
        xlt = sbt("xlt", [128, 1024], BF16)
        sq = sbt("sq", [128, 1024], F32)
        cols = sbt("cols", [128, 16], F32)
        sid = sbt("sid", [128, 128], BF16)
        sones = sbt("sones", [1, 128], F32)
        xf = sbt("xf", [128, 8 * TA], BF16)
        wz = sbt("wz", [128, 8 * 512], BF16)
        wdt = sbt("wdt", [128, 8 * 32], BF16)
        dtbs = sbt("dtbs", [1, 32], F32)
        dtbb = sbt("dtbb", [128, 32], F32)
        cwsb = sbt("cwsb", [128, 17 * 4], F32)
        cbsb = sbt("cbsb", [128, 17], F32)
        strip = sbt("strip", [128, TA], F32)
        acc = sbt("acc", [128, TA], F32)
        sbf = sbt("sbf", [128, TA], BF16)
        ztmp = sbt("ztmp", [128, 512], F32)
        dtt = sbt("dtt", [128, 32 * 5], F32)
        pZ = ps("pZ", [128, 512])
        pC = ps("pC", [128, 512])
        pD2 = ps("pD2", [128, 32])
        pB2 = ps("pB2", [128, 32])
        pT = stack.enter_context(nc.psum_tensor("pT", [128, 128], BF16))

        def f0(e):
            pr.dma(e, sid[:], idb)
            pr.dma(e, sones[:], onesf)
            pr.dma(e, cwsb[:], cw)
            pr.dma(e, cbsb[:], cb)
            pr.dma(e, dtbs[:], dtb)
            pr.dma(e, wdt[:].rearrange("p (kc j) -> p kc j", j=32),
                   win[:, 4128:4160].rearrange("(kc p) j -> p kc j", p=128))
        pr.on('sync', f0)

        def f1(e):
            pr.gate(e)
            pr.t(e.matmul(pB2[:, 0:32], sones[:], dtbs[:], start=True, stop=True))
        pr.on('tensor', f1)

        def f1b(e):
            pr.gatec(e)
            pr.t(e.copy(dtbb[:], pB2[:, 0:32]))
        pr.on('scalar', f1b)

        for r in range(NT):
            def fl(e, r=r):
                pr.gatec(e)
                pr.dma(e, xt[:], xin[r * 128:(r + 1) * 128, :])
            pr.on('sync', fl)
            ln_tile(pr, xt[:], xlt[:], {'sq': sq, 'cols': cols, 'eps': kq[1e-5]})
            for kc2 in range(8):
                def ftr(e, kc2=kc2):
                    pr.gate(e)
                    pr.t(e.transpose(pT[:, 0:128], xlt[:, kc2 * 128:(kc2 + 1) * 128], sid[:]))
                pr.on('tensor', ftr)

                def fcp(e, kc2=kc2, r=r):
                    pr.gatec(e)
                    pr.t(e.copy(xf[:, kc2 * TA + r * 128: kc2 * TA + (r + 1) * 128], pT[:, 0:128]))
                pr.on('scalar', fcp)
            def fdt(e, r=r):
                pr.gatec(e)
                for ki in range(8):
                    m = e.matmul(pD2[:, 0:32], xf[:, ki * TA + r * 128: ki * TA + (r + 1) * 128],
                                 wdt[:, ki * 32:(ki + 1) * 32], start=(ki == 0), stop=(ki == 7))
                    if ki == 7:
                        pr.t(m)
            pr.on('tensor', fdt)

            def fsp(e, r=r):
                pr.gatec(e)
                pr.t(e.tensor_add(dtt[:, 0:32], pD2[:, 0:32], dtbb[:]))
            pr.on('vector', fsp)

            def fsp2(e):
                pr.gatec(e)
                pr.t(e.activation(dtt[:, 32:64], dtt[:, 0:32], AF.Abs))
                pr.gatec(e)
                pr.t(e.activation(dtt[:, 32:64], dtt[:, 32:64], AF.Exp, scale=-1.0))
                pr.gatec(e)
                pr.t(e.activation(dtt[:, 32:64], dtt[:, 32:64], AF.Ln, bias=1.0))
            pr.on('scalar', fsp2)

            def fsp3(e, r=r):
                pr.gatec(e)
                pr.t(e.tensor_scalar_max(dtt[:, 64:96], dtt[:, 0:32], 0.0))
                pr.gatec(e)
                pr.t(e.tensor_add(dtt[:, 96:128], dtt[:, 32:64], dtt[:, 64:96]))
            pr.on('vector', fsp3)

            def fstd(e, r=r):
                pr.gatec(e)
                pr.sto(e, dt_out[r * 128:(r + 1) * 128, :], dtt[:, 96:128])
            pr.on('gpsimd', fstd)

        for zb in range(4):
            def fwz(e, zb=zb):
                pr.gatec(e)
                pr.dma(e, wz[:].rearrange("p (kc j) -> p kc j", j=512),
                       win[:, zb * 512:(zb + 1) * 512].rearrange("(kc p) j -> p kc j", p=128))
            pr.on('sync', fwz)
            for r in range(NT):
                def fmm(e, r=r, zb=zb):
                    pr.gate(e)
                    for ki in range(8):
                        m = e.matmul(pZ[:, 0:512], xf[:, ki * TA + r * 128: ki * TA + (r + 1) * 128],
                                     wz[:, ki * 512:(ki + 1) * 512], start=(ki == 0), stop=(ki == 7))
                        if ki == 7:
                            pr.t(m)
                pr.on('tensor', fmm)

                def fev(e, r=r, zb=zb):
                    pr.gatec(e)
                    pr.gates(e)
                    pr.t(e.copy(ztmp[:], pZ[:, 0:512]))
                pr.on('scalar', fev)

                def fst(e, r=r, zb=zb):
                    pr.gatec(e)
                    pr.sto(e, z_out[r * 128:(r + 1) * 128, zb * 512:(zb + 1) * 512], ztmp[:])
                pr.on('gpsimd', fst)

        for cbk in range(17):
            ncr = 128 if cbk < 16 else 32
            def fwc(e, cbk=cbk, ncr=ncr):
                pr.gatec(e)
                pr.dma(e, wz[:].rearrange("p (kc j) -> p kc j", j=512)[:, :, 0:ncr],
                       win[:, 2048 + cbk * 128: 2048 + cbk * 128 + ncr]
                       .rearrange("(kc p) j -> p kc j", p=128))
            pr.on('sync', fwc)
            for (o, w) in TGS:
                def fmm(e, cbk=cbk, o=o, w=w, ncr=ncr):
                    pr.gate(e)
                    for ki in range(8):
                        m = e.matmul(pC[0:ncr, 0:w], wz[:, ki * 512:ki * 512 + ncr],
                                     xf[:, ki * TA + o: ki * TA + o + w],
                                     start=(ki == 0), stop=(ki == 7))
                        if ki == 7:
                            pr.t(m)
                pr.on('tensor', fmm)

                def fev(e, o=o, w=w, ncr=ncr):
                    pr.gatec(e)
                    pr.t(e.copy(strip[0:ncr, o:o + w], pC[0:ncr, 0:w]))
                pr.on('scalar', fev)
            W = TA - 3

            def fcv(e, cbk=cbk):
                pr.gatec(e)
                pr.t(e.tensor_scalar(acc[:, 3:TA], strip[:, 0:W], cwsb[:, cbk * 4:cbk * 4 + 1], None, ALU.mult))
                for j in range(1, 4):
                    pr.gatec(e)
                    pr.t(e.scalar_tensor_tensor(acc[:, 3:TA], strip[:, j:j + W],
                                                cwsb[:, cbk * 4 + j:cbk * 4 + j + 1],
                                                acc[:, 3:TA], ALU.mult, ALU.add))
            pr.on('vector', fcv)

            def fsl(e, cbk=cbk):
                pr.gatec(e)
                pr.t(e.activation(sbf[:, 3:TA], acc[:, 3:TA], AF.Silu, bias=cbsb[:, cbk:cbk + 1]))
            pr.on('scalar', fsl)

            if cbk < 16:
                for r in range(NT):
                    def ftr(e, r=r):
                        pr.gate(e)
                        pr.t(e.transpose(pT[:, 0:128], sbf[:, r * 128:(r + 1) * 128], sid[:]))
                    pr.on('tensor', ftr)

                    def fcp(e, r=r):
                        pr.gatec(e)
                        pr.gates(e)
                        pr.t(e.copy(xlt[:, 0:128], pT[:, 0:128]))
                    pr.on('scalar', fcp)

                    def fst(e, r=r, cbk=cbk):
                        pr.gatec(e)
                        pr.sto(e, xh_out[r * 128:(r + 1) * 128, cbk * 128:(cbk + 1) * 128], xlt[:, 0:128])
                    pr.on('gpsimd', fst)
            else:
                def fstf(e):
                    pr.gatec(e)
                    pr.sto(e, bcf_out[:, 3:TA], sbf[0:32, 3:TA])
                pr.on('gpsimd', fstf)
                for r in range(NT):
                    def ftr(e, r=r):
                        pr.gate(e)
                        pr.t(e.transpose(pT[0:128, 0:32], sbf[0:32, r * 128:(r + 1) * 128], sid[0:32, 0:32]))
                    pr.on('tensor', ftr)

                    def fcp(e, r=r):
                        pr.gatec(e)
                        pr.gates(e)
                        pr.t(e.copy(xlt[:, 0:32], pT[:, 0:32]))
                    pr.on('scalar', fcp)

                    def fst(e, r=r):
                        pr.gatec(e)
                        pr.sto(e, bct_out[r * 128:(r + 1) * 128, :], xlt[:, 0:32])
                    pr.on('gpsimd', fst)

        def fin(e):
            e.wait_ge(st, pr.s)
        pr.on('gpsimd', fin)
    stack.close()
    return nc



def build_a3():
    nc, stack, ld, cs, st = mkprog("a3")
    yin = nc.dram_tensor("y", [TPC, 2048], F32, kind="ExternalInput").ap()
    zin = nc.dram_tensor("z", [TPC, 2048], F32, kind="ExternalInput").ap()
    xres = nc.dram_tensor("xr", [TPC, D], F32, kind="ExternalInput").ap()
    wo = nc.dram_tensor("wo", [2048, 1024], BF16, kind="ExternalInput").ap()
    nw = nc.dram_tensor("nw", [1, 2048], F32, kind="ExternalInput").ap()
    idb = nc.dram_tensor("c_idb", [128, 128], BF16, kind="ExternalInput").ap()
    idf = nc.dram_tensor("c_idf", [128, 128], F32, kind="ExternalInput").ap()
    onesf = nc.dram_tensor("c_ones", [1, 128], F32, kind="ExternalInput").ap()
    out = nc.dram_tensor("out", [TPC, D], F32, kind="ExternalOutput").ap()

    sbt = lambda n, shp, dt: stack.enter_context(nc.sbuf_tensor(n, shp, dt))
    ps = lambda n, shp: stack.enter_context(nc.psum_tensor(n, shp, F32))
    with nc.Block() as block:
        pr = P(nc, block, ld, cs, st)
        kq = mkconsts(pr, stack, nc, [1e-5])
        yt = sbt("yt", [128, 2048], F32)
        zt = sbt("zt", [128, 2048], F32)
        xt = sbt("xt", [128, 1024], F32)
        g = sbt("g", [128, 2048], F32)
        sq = sbt("sq", [128, 2048], F32)
        cols = sbt("cols", [128, 8], F32)
        sid = sbt("sid", [128, 128], BF16)
        sidf = sbt("sidf", [128, 128], F32)
        sones = sbt("sones", [1, 128], F32)
        nwr = sbt("nwr", [1, 2048], F32)
        nwb = sbt("nwb", [128, 2048], F32)
        gb = sbt("gb", [128, 2048], BF16)
        gf = sbt("gf", [128, 16 * 512], BF16)
        wos = sbt("wos", [128, 16 * 128], BF16)
        tmp = sbt("tmp", [128, 512], F32)
        otok = sbt("otok", [128, 4 * 1024], F32)
        pA = ps("pA", [128, 512])
        pTf = ps("pTf", [128, 128])
        pT = stack.enter_context(nc.psum_tensor("pT", [128, 128], BF16))

        def f0(e):
            pr.dma(e, sid[:], idb)
            pr.dma(e, sidf[:], idf)
            pr.dma(e, sones[:], onesf)
            pr.dma(e, nwr[:], nw)
        pr.on('sync', f0)

        for q in range(4):
            def fb(e, q=q):
                pr.gate(e)
                pr.t(e.matmul(pA[:, 0:512], sones[:], nwr[:, q * 512:(q + 1) * 512], start=True, stop=True))
            pr.on('tensor', fb)

            def fb2(e, q=q):
                pr.gatec(e)
                pr.t(e.copy(nwb[:, q * 512:(q + 1) * 512], pA[:, 0:512]))
            pr.on('scalar', fb2)

        for grp in range(8):
            for r4 in range(4):
                r = grp * 4 + r4

                def fl(e, r=r):
                    pr.gatec(e)
                    pr.dma(e, yt[:], yin[r * 128:(r + 1) * 128, :])
                    pr.dma(e, zt[:], zin[r * 128:(r + 1) * 128, :])
                pr.on('sync', fl)

                def fz(e):
                    pr.gate(e)
                    pr.t(e.activation(sq[:], zt[:], AF.Silu))
                pr.on('scalar', fz)

                def fg(e):
                    pr.gatec(e)
                    pr.t(e.tensor_mul(g[:], yt[:], sq[:]))
                pr.on('vector', fg)

                def fsq(e):
                    pr.gatec(e)
                    pr.t(e.activation(sq[:], g[:], AF.Square))
                pr.on('scalar', fsq)

                def fms(e):
                    pr.gatec(e)
                    pr.t(e.reduce_sum(cols[:, 0:1], sq[:], AX.X))
                    pr.gatec(e)
                    pr.t(e.tensor_scalar_mul(cols[:, 0:1], cols[:, 0:1], 1.0 / 2048))
                pr.on('vector', fms)

                def frs(e):
                    pr.gatec(e)
                    pr.t(e.activation(cols[:, 1:2], cols[:, 0:1], AF.Ln, bias=kq[1e-5]))
                    pr.gatec(e)
                    pr.t(e.activation(cols[:, 1:2], cols[:, 1:2], AF.Exp, scale=-0.5))
                pr.on('scalar', frs)

                def fnw(e):
                    pr.gatec(e)
                    pr.t(e.tensor_scalar(g[:], g[:], cols[:, 1:2], None, ALU.mult))
                    pr.gatec(e)
                    pr.t(e.tensor_mul(gb[:], g[:], nwb[:]))
                pr.on('vector', fnw)

                for kc2 in range(16):
                    def ftr(e, kc2=kc2):
                        pr.gate(e)
                        pr.t(e.transpose(pT[:, 0:128], gb[:, kc2 * 128:(kc2 + 1) * 128], sid[:]))
                    pr.on('tensor', ftr)

                    def fcp(e, kc2=kc2, r4=r4):
                        pr.gatec(e)
                        pr.t(e.copy(gf[:, kc2 * 512 + r4 * 128: kc2 * 512 + (r4 + 1) * 128], pT[:, 0:128]))
                    pr.on('scalar', fcp)

            for do in range(8):
                def fw(e, do=do):
                    pr.gatec(e)
                    pr.dma(e, wos[:].rearrange("p (kc j) -> p kc j", j=128),
                           wo[:, do * 128:(do + 1) * 128].rearrange("(kc p) j -> p kc j", p=128))
                pr.on('sync', fw)

                def fmm(e, do=do):
                    pr.gate(e)
                    for ki in range(16):
                        m = e.matmul(pA[:, 0:512], wos[:, ki * 128:(ki + 1) * 128],
                                     gf[:, ki * 512:(ki + 1) * 512], start=(ki == 0), stop=(ki == 15))
                        if ki == 15:
                            pr.t(m)
                pr.on('tensor', fmm)

                def fev(e, do=do):
                    pr.gatec(e)
                    pr.t(e.copy(tmp[:], pA[:, 0:512]))
                pr.on('scalar', fev)

                for q in range(4):
                    def ftr(e, q=q):
                        pr.gate(e)
                        pr.t(e.transpose(pTf[:, 0:128], tmp[:, q * 128:(q + 1) * 128], sidf[:]))
                    pr.on('tensor', ftr)

                    def fcp2(e, q=q, do=do):
                        pr.gatec(e)
                        pr.gates(e)
                        pr.t(e.tensor_copy(otok[:, q * 1024 + do * 128: q * 1024 + (do + 1) * 128], pTf[:, 0:128]))
                    pr.on('vector', fcp2)

            for q in range(4):
                r = grp * 4 + q

                def flx(e, r=r):
                    pr.gatec(e)
                    pr.dma(e, xt[:], xres[r * 128:(r + 1) * 128, :])
                pr.on('sync', flx)

                def fadd(e, q=q):
                    pr.gate(e)
                    pr.t(e.tensor_add(otok[:, q * 1024:(q + 1) * 1024], otok[:, q * 1024:(q + 1) * 1024], xt[:]))
                pr.on('vector', fadd)

                def fst(e, r=r, q=q):
                    pr.gatec(e)
                    pr.sto(e, out[r * 128:(r + 1) * 128, :], otok[:, q * 1024:(q + 1) * 1024])
                pr.on('gpsimd', fst)

        def fin(e):
            e.wait_ge(st, pr.s)
        pr.on('gpsimd', fin)
    stack.close()
    return nc



def build_a2():
    nc, stack, ld, cs, st = mkprog("a2")
    SQ = SEQ
    NCH = SQ // 64
    xh = nc.dram_tensor("xh", [SQ, 1024], BF16, kind="ExternalInput").ap()
    bct = nc.dram_tensor("bct", [SQ, 32], BF16, kind="ExternalInput").ap()
    bcf = nc.dram_tensor("bcf", [32, SQ], BF16, kind="ExternalInput").ap()
    dt = nc.dram_tensor("dt", [SQ, 16], F32, kind="ExternalInput").ap()
    alog = nc.dram_tensor("alog", [1, 16], F32, kind="ExternalInput").ap()
    dv = nc.dram_tensor("dv", [1, 16], F32, kind="ExternalInput").ap()
    triu = nc.dram_tensor("c_triu", [64, 64], F32, kind="ExternalInput").ap()
    mneg = nc.dram_tensor("c_mneg", [64, 64], F32, kind="ExternalInput").ap()
    sel = nc.dram_tensor("c_sel", [16, 16 * 64], F32, kind="ExternalInput").ap()
    onesf = nc.dram_tensor("c_ones", [1, 128], F32, kind="ExternalInput").ap()
    idb = nc.dram_tensor("c_idb", [128, 128], BF16, kind="ExternalInput").ap()
    idf = nc.dram_tensor("c_idf", [128, 128], F32, kind="ExternalInput").ap()
    y_out = nc.dram_tensor("y_out", [SQ, 1024], F32, kind="ExternalOutput").ap()

    sbt = lambda n, shp, dt_: stack.enter_context(nc.sbuf_tensor(n, shp, dt_))
    ps = lambda n, shp: stack.enter_context(nc.psum_tensor(n, shp, F32))
    with nc.Block() as block:
        pr = P(nc, block, ld, cs, st)
        struw = sbt("struw", [64, 64], F32)
        smneg = sbt("smneg", [64, 64], F32)
        ssel = sbt("ssel", [16, 16 * 64], F32)
        sones = sbt("sones", [1, 128], F32)
        sid = sbt("sid", [128, 128], BF16)
        sidf = sbt("sidf", [128, 128], F32)
        salog = sbt("salog", [1, 16], F32)
        sdv = sbt("sdv", [1, 16], F32)
        arow = sbt("arow", [1, 16], F32)
        A_bc = sbt("A_bc", [64, 16], F32)
        D_bc = sbt("D_bc", [64, 16], F32)
        xhc = sbt("xhc", [64, 1024], BF16)
        xcd = sbt("xcd", [64, 1024], BF16)
        bctc = sbt("bctc", [64, 32], BF16)
        bfm = sbt("bfm", [16, 64], BF16)
        cfm = sbt("cfm", [16, 64], BF16)
        dtc = sbt("dtc", [64, 16], F32)
        a_sb = sbt("a_sb", [64, 16], F32)
        cs_sb = sbt("cs_sb", [64, 16], F32)
        tot_sb = sbt("tot_sb", [1, 16], F32)
        etot = sbt("etot", [1, 16], F32)
        exp_cs = sbt("exp_cs", [64, 16], F32)
        decay = sbt("decay", [64, 16], F32)
        drep = sbt("drep", [16, 16], F32)
        csT = sbt("csT", [16, 64], F32)
        Gt_sb = sbt("Gt_sb", [64, 64], F32)
        diff = sbt("diff", [64, 64], F32)
        Lt = sbt("Lt", [64, 64], F32)
        Ph = sbt("Ph", [64, 64], BF16)
        Ccs = sbt("Ccs", [64, 16], BF16)
        cctS = sbt("cctS", [16, 64], BF16)
        Bcs = sbt("Bcs", [64, 16], BF16)
        R_f = sbt("R_f", [16, 1024], F32)
        R_b = sbt("R_b", [16, 1024], BF16)
        y_sb = sbt("y_sb", [64, 1024], F32)
        pP = ps("pP", [64, 64])
        pGt = ps("pGt", [64, 64])
        pBC = ps("pBC", [64, 64])
        pY = ps("pY", [64, 64])
        pS = ps("pS", [16, 64])
        pTb = stack.enter_context(nc.psum_tensor("pTb", [16, 64], BF16))

        def f0(e):
            pr.dma(e, struw[:], triu)
            pr.dma(e, smneg[:], mneg)
            pr.dma(e, ssel[:], sel)
            pr.dma(e, sones[:], onesf)
            pr.dma(e, sid[:], idb)
            pr.dma(e, sidf[:], idf)
            pr.dma(e, salog[:], alog)
            pr.dma(e, sdv[:], dv)
        pr.on('sync', f0)

        def fz(e):
            pr.gate(e)
            pr.t(e.memset(R_f[:], 0.0))
            pr.gatec(e)
            pr.t(e.memset(R_b[:], 0.0))
        pr.on('vector', fz)

        def fa(e):
            pr.gate(e)
            pr.t(e.activation(arow[:], salog[:], AF.Exp))
        pr.on('scalar', fa)

        def fa2(e):
            pr.gatec(e)
            pr.t(e.tensor_scalar_mul(arow[:], arow[:], -1.0))
        pr.on('vector', fa2)

        def fa3(e):
            pr.gatec(e)
            pr.t(e.matmul(pP[:, 0:16], sones[0:1, 0:64], arow[:], start=True, stop=True))
        pr.on('tensor', fa3)

        def fa4(e):
            pr.gatec(e)
            pr.t(e.copy(A_bc[:], pP[:, 0:16]))
        pr.on('scalar', fa4)

        def fa5(e):
            pr.gatec(e)
            pr.t(e.matmul(pP[:, 0:16], sones[0:1, 0:64], sdv[:], start=True, stop=True))
        pr.on('tensor', fa5)

        def fa6(e):
            pr.gatec(e)
            pr.t(e.copy(D_bc[:], pP[:, 0:16]))
        pr.on('scalar', fa6)

        ones64 = sbt("ones64", [64, 1], F32)

        def fo64(e):
            pr.gatec(e)
            pr.t(e.memset(ones64[:], 1.0))
        pr.on('vector', fo64)

        for k in range(NCH):
            def fl(e, k=k):
                pr.gatec(e)
                pr.gates(e)
                pr.dma(e, xhc[:], xh[k * 64:(k + 1) * 64, :])
                pr.dma(e, bctc[:], bct[k * 64:(k + 1) * 64, :])
                pr.dma(e, bfm[:], bcf[0:16, k * 64:(k + 1) * 64])
                pr.dma(e, cfm[:], bcf[16:32, k * 64:(k + 1) * 64])
                pr.dma(e, dtc[:], dt[k * 64:(k + 1) * 64, :])
            pr.on('sync', fl)

            def fp1(e):
                pr.gate(e)
                pr.t(e.tensor_mul(a_sb[:], dtc[:], A_bc[:]))
            pr.on('vector', fp1)

            def fp2(e):
                pr.gatec(e)
                pr.t(e.matmul(pP[:, 0:16], struw[:], a_sb[:], start=True, stop=True))
                pr.gatec(e)
                pr.t(e.matmul(pBC[0:1, 0:16], ones64[:], a_sb[:], start=True, stop=True))
            pr.on('tensor', fp2)

            def fp3(e):
                pr.gatec(e)
                pr.t(e.copy(cs_sb[:], pP[:, 0:16]))
                pr.gatec(e)
                pr.t(e.copy(tot_sb[:], pBC[0:1, 0:16]))
                pr.gatec(e)
                pr.t(e.activation(exp_cs[:], cs_sb[:], AF.Exp))
                pr.gatec(e)
                pr.t(e.activation(etot[:], tot_sb[:], AF.Exp))
            pr.on('scalar', fp3)

            def fp4(e):
                pr.gatec(e)
                pr.t(e.matmul(pP[:, 0:16], sones[0:1, 0:64], tot_sb[:], start=True, stop=True))
                pr.gatec(e)
                pr.t(e.matmul(pBC[0:16, 0:16], sones[0:1, 0:16], etot[:], start=True, stop=True))
            pr.on('tensor', fp4)

            def fp5(e):
                pr.gatec(e)
                pr.t(e.tensor_sub(decay[:], pP[:, 0:16], cs_sb[:]))
                pr.gatec(e)
                pr.t(e.tensor_copy(drep[:], pBC[0:16, 0:16]))
            pr.on('vector', fp5)

            def fp6(e):
                pr.gatec(e)
                pr.t(e.activation(decay[:], decay[:], AF.Exp))
            pr.on('scalar', fp6)

            def fp7(e):
                pr.gatec(e)
                pr.t(e.transpose(pP[0:16, 0:64], cs_sb[:], sidf[0:64, 0:64]))
            pr.on('tensor', fp7)

            def fp8(e):
                pr.gatec(e)
                pr.t(e.copy(csT[:], pP[0:16, 0:64]))
            pr.on('scalar', fp8)

            def fp9(e):
                pr.gatec(e)
                pr.t(e.matmul(pGt[:, 0:64], bfm[:], cfm[:], start=True, stop=True))
            pr.on('tensor', fp9)

            def fp10(e):
                pr.gatec(e)
                pr.t(e.copy(Gt_sb[:], pGt[:, 0:64]))
            pr.on('scalar', fp10)

            for h in range(16):
                def g1(e, h=h):
                    pr.gatec(e)
                    pr.t(e.tensor_scalar(xcd[:, h * 64:(h + 1) * 64], xhc[:, h * 64:(h + 1) * 64],
                                         dtc[:, h:h + 1], None, ALU.mult))
                    pr.gatec(e)
                    pr.t(e.tensor_scalar(Ccs[:], bctc[:, 16:32], exp_cs[:, h:h + 1], None, ALU.mult))
                    pr.gatec(e)
                    pr.t(e.tensor_scalar(Bcs[:], bctc[:, 0:16], decay[:, h:h + 1], None, ALU.mult))
                pr.on('vector', g1)

                def g2(e, h=h):
                    pr.gatec(e)
                    pr.t(e.matmul(pBC[:, 0:64], ssel[:, h * 64:(h + 1) * 64], csT[:], start=True, stop=True))
                    pr.gatec(e)
                    pr.t(e.transpose(pTb[:, 0:64], Ccs[:], sid[0:64, 0:64]))
                pr.on('tensor', g2)

                def g3(e, h=h):
                    pr.gatec(e)
                    pr.t(e.scalar_tensor_tensor(diff[:], pBC[:, 0:64], cs_sb[:, h:h + 1], smneg[:],
                                                ALU.subtract, ALU.add))
                pr.on('vector', g3)

                def g4(e, h=h):
                    pr.gatec(e)
                    pr.t(e.activation(Lt[:], diff[:], AF.Exp))
                    pr.gatec(e)
                    pr.t(e.copy(cctS[:], pTb[:, 0:64]))
                pr.on('scalar', g4)

                def g5(e, h=h):
                    pr.gatec(e)
                    pr.t(e.tensor_mul(Ph[:], Lt[:], Gt_sb[:]))
                pr.on('vector', g5)

                def g6(e, h=h):
                    pr.gatec(e)
                    e.matmul(pY[:, 0:64], Ph[:], xcd[:, h * 64:(h + 1) * 64], start=True, stop=False)
                    pr.t(e.matmul(pY[:, 0:64], cctS[:], R_b[:, h * 64:(h + 1) * 64], start=False, stop=True))
                    pr.gatec(e)
                    pr.t(e.matmul(pS[:, 0:64], Bcs[:], xcd[:, h * 64:(h + 1) * 64], start=True, stop=True))
                pr.on('tensor', g6)

                def g7(e, h=h):
                    pr.gatec(e)
                    pr.t(e.scalar_tensor_tensor(y_sb[:, h * 64:(h + 1) * 64], xhc[:, h * 64:(h + 1) * 64],
                                                D_bc[:, h:h + 1], pY[:, 0:64], ALU.mult, ALU.add))
                    pr.gatec(e)
                    pr.t(e.scalar_tensor_tensor(R_f[:, h * 64:(h + 1) * 64], R_f[:, h * 64:(h + 1) * 64],
                                                drep[:, h:h + 1], pS[:, 0:64], ALU.mult, ALU.add))
                pr.on('vector', g7)

                def g8(e, h=h):
                    pr.gatec(e)
                    pr.t(e.copy(R_b[:, h * 64:(h + 1) * 64], R_f[:, h * 64:(h + 1) * 64]))
                pr.on('scalar', g8)

            def fst(e, k=k):
                pr.gatec(e)
                pr.sto(e, y_out[k * 64:(k + 1) * 64, :], y_sb[:])
            pr.on('gpsimd', fst)

        def fin(e):
            e.wait_ge(st, pr.s)
        pr.on('gpsimd', fin)
    stack.close()
    return nc


_a1_nc = _a2_nc = _a3_nc = None


def run_mamba(x1_full, wb, raw):
    global _a1_nc, _a2_nc, _a3_nc
    if _a1_nc is None:
        _a1_nc = build_a1()
    if _a2_nc is None:
        _a2_nc = build_a2()
    if _a3_nc is None:
        _a3_nc = build_a3()
    idb = np.eye(128, dtype=BF)
    idf = np.eye(128, dtype=np.float32)
    ones = np.ones((1, 128), dtype=np.float32)
    cw = np.zeros((2176, 4), np.float32)
    cw[:2080] = raw['conv_w']
    cw = np.ascontiguousarray(cw.reshape(17, 128, 4).transpose(1, 0, 2).reshape(128, 68))
    cbv = np.zeros((2176,), np.float32)
    cbv[:2080] = raw['conv_b']
    cbv = np.ascontiguousarray(cbv.reshape(17, 128).T)
    dtb = raw['dt_bias'].reshape(1, 32).astype(np.float32)

    im = []
    for c in range(8):
        t0 = c * TPC
        xa = np.zeros((TA, D), np.float32)
        xa[128:] = x1_full[t0:t0 + TPC]
        if c % 2 == 1:
            xa[:128] = x1_full[t0 - 128:t0]
        im.append({"x": xa, "win": wb['l0_winT'], "cw": cw, "cb": cbv, "dtb": dtb,
                   "c_idb": idb, "c_ones": ones})
    r1 = run_bass_kernel_spmd(_a1_nc, im, core_ids=list(range(8)))
    r1 = r1.results

    xh_b, bct_b, bcf_b, dt_b, z_c = [], [], [], [], []
    for b in range(4):
        e_, o_ = r1[2 * b], r1[2 * b + 1]
        xh_b.append(np.concatenate([e_["xh_out"][128:], o_["xh_out"][128:]], axis=0))
        bct_b.append(np.concatenate([e_["bct_out"][128:], o_["bct_out"][128:]], axis=0))
        bcf_b.append(np.concatenate([e_["bcf_out"][:, 128:], o_["bcf_out"][:, 128:]], axis=1))
        dt_b.append(np.concatenate([e_["dt_out"][128:], o_["dt_out"][128:]], axis=0))
    for c in range(8):
        z_c.append(r1[c]["z_out"][128:])

    alog = raw['A_log'].astype(np.float32)
    dvv = raw['D'].astype(np.float32)
    triu = np.triu(np.ones((64, 64), np.float32))
    mneg = np.where(np.arange(64)[None, :] >= np.arange(64)[:, None], 0.0, -1e9).astype(np.float32)
    sel = np.zeros((16, 16 * 64), np.float32)
    for h in range(16):
        sel[h, h * 64:(h + 1) * 64] = 1.0
    im2 = []
    for c2 in range(8):
        b, hh = c2 // 2, c2 % 2
        im2.append({
            "xh": np.ascontiguousarray(xh_b[b][:, hh * 1024:(hh + 1) * 1024]),
            "bct": bct_b[b], "bcf": bcf_b[b],
            "dt": np.ascontiguousarray(dt_b[b][:, hh * 16:(hh + 1) * 16]),
            "alog": np.ascontiguousarray(alog[hh * 16:(hh + 1) * 16].reshape(1, 16)),
            "dv": np.ascontiguousarray(dvv[hh * 16:(hh + 1) * 16].reshape(1, 16)),
            "c_triu": triu, "c_mneg": mneg, "c_sel": sel, "c_ones": ones,
            "c_idb": idb, "c_idf": idf})
    r2 = run_bass_kernel_spmd(_a2_nc, im2, core_ids=list(range(8)))
    r2 = r2.results
    y_full = np.zeros((TT, 2048), np.float32)
    for b in range(4):
        y_full[b * SEQ:(b + 1) * SEQ, 0:1024] = r2[2 * b]["y_out"]
        y_full[b * SEQ:(b + 1) * SEQ, 1024:2048] = r2[2 * b + 1]["y_out"]

    nwrow = raw['norm_w'].reshape(1, 2048).astype(np.float32)
    im3 = []
    for c in range(8):
        t0 = c * TPC
        im3.append({"y": np.ascontiguousarray(y_full[t0:t0 + TPC]), "z": z_c[c],
                    "xr": np.ascontiguousarray(x1_full[t0:t0 + TPC]),
                    "wo": wb['l0_woutT'], "nw": nwrow,
                    "c_idb": idb, "c_idf": idf, "c_ones": ones})
    r3 = run_bass_kernel_spmd(_a3_nc, im3, core_ids=list(range(8)))
    return np.concatenate([r["out"] for r in r3.results], axis=0)



def build_b1():
    nc, stack, ld, cs, st = mkprog("b1")
    xin = nc.dram_tensor("x", [TA, D], F32, kind="ExternalInput").ap()
    wq = nc.dram_tensor("wq", [1024, 1024], BF16, kind="ExternalInput").ap()
    wk = nc.dram_tensor("wk", [1024, 1024], BF16, kind="ExternalInput").ap()
    wv = nc.dram_tensor("wv", [1024, 1024], BF16, kind="ExternalInput").ap()
    wbv = nc.dram_tensor("wbv", [1024, 4], BF16, kind="ExternalInput").ap()
    cq = nc.dram_tensor("cq", [128, 8 * 4], F32, kind="ExternalInput").ap()
    ck = nc.dram_tensor("ck", [128, 8 * 4], F32, kind="ExternalInput").ap()
    cv = nc.dram_tensor("cv", [128, 8 * 4], F32, kind="ExternalInput").ap()
    idb = nc.dram_tensor("c_idb", [128, 128], BF16, kind="ExternalInput").ap()
    qf_o = nc.dram_tensor("qf", [1024, TA], BF16, kind="ExternalOutput").ap()
    kf_o = nc.dram_tensor("kf", [1024, TA], BF16, kind="ExternalOutput").ap()
    vf_o = nc.dram_tensor("vf", [1024, TA], BF16, kind="ExternalOutput").ap()
    qt_o = nc.dram_tensor("qt", [TA, 1024], BF16, kind="ExternalOutput").ap()
    kt_o = nc.dram_tensor("kt", [TA, 1024], BF16, kind="ExternalOutput").ap()
    vt_o = nc.dram_tensor("vt", [TA, 1024], BF16, kind="ExternalOutput").ap()
    bt_o = nc.dram_tensor("bt", [TA, 4], F32, kind="ExternalOutput").ap()

    NT = TA // 128
    TGS = [(g * 512, 512) for g in range(TA // 512)]
    if TA % 512:
        TGS.append((TA - TA % 512, TA % 512))

    sbt = lambda n, shp, dt_: stack.enter_context(nc.sbuf_tensor(n, shp, dt_))
    ps = lambda n, shp: stack.enter_context(nc.psum_tensor(n, shp, F32))
    with nc.Block() as block:
        pr = P(nc, block, ld, cs, st)
        kq = mkconsts(pr, stack, nc, [1e-5])
        xt = sbt("xt", [128, 1024], F32)
        xlt = sbt("xlt", [128, 1024], BF16)
        sq = sbt("sq", [128, 1024], F32)
        cols = sbt("cols", [128, 16], F32)
        sid = sbt("sid", [128, 128], BF16)
        xf = sbt("xf", [128, 8 * TA], BF16)
        wsb = sbt("wsb", [128, 8 * 128], BF16)
        wbsb = sbt("wbsb", [128, 8 * 4], BF16)
        cwsb = sbt("cwsb", [128, 3 * 32], F32)
        strip = sbt("strip", [128, TA], F32)
        acc = sbt("acc", [128, TA], F32)
        sbf = sbt("sbf", [128, TA], BF16)
        ttile = sbt("ttile", [128, 128], BF16)
        btile = sbt("btile", [128, 4], F32)
        pC = ps("pC", [128, 512])
        pB2 = ps("pB2", [128, 4])
        pT = stack.enter_context(nc.psum_tensor("pT", [128, 128], BF16))

        def f0(e):
            pr.dma(e, sid[:], idb)
            pr.dma(e, cwsb[:, 0:32], cq)
            pr.dma(e, cwsb[:, 32:64], ck)
            pr.dma(e, cwsb[:, 64:96], cv)
            pr.dma(e, wbsb[:].rearrange("p (kc j) -> p kc j", j=4),
                   wbv.rearrange("(kc p) j -> p kc j", p=128))
        pr.on('sync', f0)

        for r in range(NT):
            def fl(e, r=r):
                pr.gatec(e)
                pr.dma(e, xt[:], xin[r * 128:(r + 1) * 128, :])
            pr.on('sync', fl)
            ln_tile(pr, xt[:], xlt[:], {'sq': sq, 'cols': cols, 'eps': kq[1e-5]})
            for kc2 in range(8):
                def ftr(e, kc2=kc2):
                    pr.gate(e)
                    pr.t(e.transpose(pT[:, 0:128], xlt[:, kc2 * 128:(kc2 + 1) * 128], sid[:]))
                pr.on('tensor', ftr)

                def fcp(e, kc2=kc2, r=r):
                    pr.gatec(e)
                    pr.t(e.copy(xf[:, kc2 * TA + r * 128: kc2 * TA + (r + 1) * 128], pT[:, 0:128]))
                pr.on('scalar', fcp)
            def fbm(e, r=r):
                pr.gatec(e)
                for ki in range(8):
                    m = e.matmul(pB2[:, 0:4], xf[:, ki * TA + r * 128: ki * TA + (r + 1) * 128],
                                 wbsb[:, ki * 4:(ki + 1) * 4], start=(ki == 0), stop=(ki == 7))
                    if ki == 7:
                        pr.t(m)
            pr.on('tensor', fbm)

            def fbs(e, r=r):
                pr.gatec(e)
                pr.gates(e)
                pr.t(e.activation(btile[:], pB2[:, 0:4], AF.Sigmoid))
            pr.on('scalar', fbs)

            def fbst(e, r=r):
                pr.gatec(e)
                pr.sto(e, bt_o[r * 128:(r + 1) * 128, :], btile[:])
            pr.on('gpsimd', fbst)

        for (wmat, cvoff, fo, to) in ((wq, 0, qf_o, qt_o), (wk, 32, kf_o, kt_o), (wv, 64, vf_o, vt_o)):
            for cbk in range(8):
                def fwc(e, cbk=cbk, wmat=wmat):
                    pr.gatec(e)
                    pr.dma(e, wsb[:].rearrange("p (kc j) -> p kc j", j=128),
                           wmat[:, cbk * 128:(cbk + 1) * 128].rearrange("(kc p) j -> p kc j", p=128))
                pr.on('sync', fwc)
                for (o, w) in TGS:
                    def fmm(e, cbk=cbk, o=o, w=w):
                        pr.gate(e)
                        for ki in range(8):
                            m = e.matmul(pC[:, 0:w], wsb[:, ki * 128:(ki + 1) * 128],
                                         xf[:, ki * TA + o: ki * TA + o + w],
                                         start=(ki == 0), stop=(ki == 7))
                            if ki == 7:
                                pr.t(m)
                    pr.on('tensor', fmm)

                    def fev(e, o=o, w=w):
                        pr.gatec(e)
                        pr.t(e.copy(strip[:, o:o + w], pC[:, 0:w]))
                    pr.on('scalar', fev)
                W = TA - 3

                def fcv(e, cbk=cbk, cvoff=cvoff):
                    pr.gatec(e)
                    pr.t(e.tensor_scalar(acc[:, 3:TA], strip[:, 0:W],
                                         cwsb[:, cvoff + cbk * 4:cvoff + cbk * 4 + 1], None, ALU.mult))
                    for j in range(1, 4):
                        pr.gatec(e)
                        pr.t(e.scalar_tensor_tensor(acc[:, 3:TA], strip[:, j:j + W],
                                                    cwsb[:, cvoff + cbk * 4 + j:cvoff + cbk * 4 + j + 1],
                                                    acc[:, 3:TA], ALU.mult, ALU.add))
                pr.on('vector', fcv)

                def fsl(e):
                    pr.gatec(e)
                    pr.gates(e)
                    pr.t(e.activation(sbf[:, 3:TA], acc[:, 3:TA], AF.Silu))
                pr.on('scalar', fsl)

                def fstf(e, cbk=cbk, fo=fo):
                    pr.gatec(e)
                    pr.sto(e, fo[cbk * 128:(cbk + 1) * 128, 3:TA], sbf[:, 3:TA])
                pr.on('gpsimd', fstf)
                for r in range(NT):
                    def ftr(e, r=r):
                        pr.gate(e)
                        pr.t(e.transpose(pT[:, 0:128], sbf[:, r * 128:(r + 1) * 128], sid[:]))
                    pr.on('tensor', ftr)

                    def fcp(e, r=r):
                        pr.gatec(e)
                        pr.gates(e)
                        pr.t(e.copy(ttile[:], pT[:, 0:128]))
                    pr.on('scalar', fcp)

                    def fst(e, r=r, cbk=cbk, to=to):
                        pr.gatec(e)
                        pr.sto(e, to[r * 128:(r + 1) * 128, cbk * 128:(cbk + 1) * 128], ttile[:])
                    pr.on('gpsimd', fst)

        def fin(e):
            e.wait_ge(st, pr.s)
        pr.on('gpsimd', fin)
    stack.close()
    return nc


FIRK = [1, 3, 7, 15, 31]
FIROFF = [0, 1, 4, 11, 26]


def build_b3():
    nc, stack, ld, cs, st = mkprog("b3")
    xin = nc.dram_tensor("x", [TPC, D], F32, kind="ExternalInput").ap()
    vf = nc.dram_tensor("vf", [1024, TA], BF16, kind="ExternalInput").ap()
    vt = nc.dram_tensor("vt", [TA, 1024], BF16, kind="ExternalInput").ap()
    dl = nc.dram_tensor("dl", [TPC, 1024], F32, kind="ExternalInput").ap()
    fw = nc.dram_tensor("fw", [128, 8 * 57], F32, kind="ExternalInput").ap()
    gw1 = nc.dram_tensor("gw1", [1052, 2048], BF16, kind="ExternalInput").ap()
    gb1 = nc.dram_tensor("gb1", [128, 16], F32, kind="ExternalInput").ap()
    gw2 = nc.dram_tensor("gw2", [2048, 28], BF16, kind="ExternalInput").ap()
    gb2 = nc.dram_tensor("gb2", [1, 28], F32, kind="ExternalInput").ap()
    nw = nc.dram_tensor("nw", [1, 1024], F32, kind="ExternalInput").ap()
    wo = nc.dram_tensor("wo", [1024, 1024], BF16, kind="ExternalInput").ap()
    idb = nc.dram_tensor("c_idb", [128, 128], BF16, kind="ExternalInput").ap()
    idf = nc.dram_tensor("c_idf", [128, 128], F32, kind="ExternalInput").ap()
    onesf = nc.dram_tensor("c_ones", [1, 128], F32, kind="ExternalInput").ap()
    out = nc.dram_tensor("out", [TPC, D], F32, kind="ExternalOutput").ap()

    sbt = lambda n, shp, dt_: stack.enter_context(nc.sbuf_tensor(n, shp, dt_))
    ps = lambda n, shp: stack.enter_context(nc.psum_tensor(n, shp, F32))
    with nc.Block() as block:
        pr = P(nc, block, ld, cs, st)
        kq = mkconsts(pr, stack, nc, [1e-5, 1e-6, 0.02])
        xt4 = sbt("xt4", [128, 4 * 1024], F32)
        xlt = sbt("xlt", [128, 1024], BF16)
        sq = sbt("sq", [128, 1024], F32)
        cols = sbt("cols", [128, 16], F32)
        sid = sbt("sid", [128, 128], BF16)
        sidf = sbt("sidf", [128, 128], F32)
        sones = sbt("sones", [1, 128], F32)
        fwsb = sbt("fwsb", [128, 8 * 57], F32)
        gb1s = sbt("gb1s", [128, 16], F32)
        gb2s = sbt("gb2s", [1, 28], F32)
        b2bc = sbt("b2bc", [128, 28], F32)
        nwr = sbt("nwr", [1, 1024], F32)
        nwb = sbt("nwb", [128, 1024], F32)
        gif = sbt("gif", [128, 9 * 512], BF16)
        strip = sbt("strip", [128, 544], BF16)
        acc = sbt("acc", [128, 512], F32)
        fbf = sbt("fbf", [128, 512], BF16)
        firt = sbt("firt", [128, 5 * 4 * 1024], BF16)
        delt = sbt("delt", [128, 4 * 1024], F32)
        v4t = sbt("v4t", [128, 4 * 1024], BF16)
        stats = sbt("stats", [128, 4 * 28], F32)
        statb = sbt("statb", [128, 28], BF16)
        gw1s = sbt("gw1s", [128, 8 * 128], BF16)
        gw1c = sbt("gw1c", [28, 128], BF16)
        gw2s = sbt("gw2s", [128, 16 * 28], BF16)
        hmid = sbt("hmid", [128, 16 * 512], BF16)
        wts = sbt("wts", [128, 4 * 28], F32)
        spt = sbt("spt", [128, 28 * 3], F32)
        ot = sbt("ot", [128, 1024], F32)
        ob = sbt("ob", [128, 1024], BF16)
        off_ = sbt("off", [128, 8 * 512], BF16)
        wosb = sbt("wosb", [128, 8 * 128], BF16)
        tmp = sbt("tmp", [128, 512], F32)
        otok4 = sbt("otok4", [128, 4 * 1024], F32)
        pA = ps("pA", [128, 512])
        pL = ps("pL", [128, 28])
        pTf = ps("pTf", [128, 128])
        pT = stack.enter_context(nc.psum_tensor("pT", [128, 128], BF16))

        def f0(e):
            pr.dma(e, sid[:], idb)
            pr.dma(e, sidf[:], idf)
            pr.dma(e, sones[:], onesf)
            pr.dma(e, fwsb[:], fw)
            pr.dma(e, gb1s[:], gb1)
            pr.dma(e, gb2s[:], gb2)
            pr.dma(e, nwr[:], nw)
            pr.dma(e, gw2s[:].rearrange("p (kc j) -> p kc j", j=28),
                   gw2.rearrange("(kc p) j -> p kc j", p=128))
        pr.on('sync', f0)

        def fb(e):
            pr.gate(e)
            pr.t(e.matmul(pL[:, 0:28], sones[:], gb2s[:], start=True, stop=True))
        pr.on('tensor', fb)

        def fb2(e):
            pr.gatec(e)
            pr.t(e.copy(b2bc[:], pL[:, 0:28]))
        pr.on('scalar', fb2)
        for q in range(2):
            def fb3(e, q=q):
                pr.gatec(e)
                pr.t(e.matmul(pA[:, 0:512], sones[:], nwr[:, q * 512:(q + 1) * 512], start=True, stop=True))
            pr.on('tensor', fb3)

            def fb4(e, q=q):
                pr.gatec(e)
                pr.t(e.copy(nwb[:, q * 512:(q + 1) * 512], pA[:, 0:512]))
            pr.on('scalar', fb4)

        for blk in range(TPC // 512):
            t0h = 128 + blk * 512
            for r4 in range(4):
                def fl(e, r4=r4, blk=blk, t0h=t0h):
                    pr.gatec(e)
                    pr.gates(e)
                    pr.dma(e, xt4[:, r4 * 1024:(r4 + 1) * 1024],
                           xin[blk * 512 + r4 * 128: blk * 512 + (r4 + 1) * 128, :])
                    pr.dma(e, delt[:, r4 * 1024:(r4 + 1) * 1024],
                           dl[blk * 512 + r4 * 128: blk * 512 + (r4 + 1) * 128, :])
                    pr.dma(e, v4t[:, r4 * 1024:(r4 + 1) * 1024],
                           vt[t0h + r4 * 128: t0h + (r4 + 1) * 128, :])
                pr.on('sync', fl)
                ln_tile(pr, xt4[:, r4 * 1024:(r4 + 1) * 1024], xlt[:],
                        {'sq': sq, 'cols': cols, 'eps': kq[1e-5]})
                for kc2 in range(8):
                    def ftr(e, kc2=kc2):
                        pr.gate(e)
                        pr.t(e.transpose(pT[:, 0:128], xlt[:, kc2 * 128:(kc2 + 1) * 128], sid[:]))
                    pr.on('tensor', ftr)

                    def fcp(e, kc2=kc2, r4=r4):
                        pr.gatec(e)
                        pr.t(e.copy(gif[:, kc2 * 512 + r4 * 128: kc2 * 512 + (r4 + 1) * 128], pT[:, 0:128]))
                    pr.on('scalar', fcp)

            for cb in range(8):
                def fls(e, cb=cb, t0h=t0h):
                    pr.gatec(e)
                    pr.dma(e, strip[:], vf[cb * 128:(cb + 1) * 128, t0h - 32: t0h + 512])
                pr.on('sync', fls)
                for s, K in enumerate(FIRK):
                    def fcv(e, s=s, K=K, cb=cb):
                        pr.gate(e)
                        base = cb * 57 + FIROFF[s]
                        pr.t(e.tensor_scalar(acc[:], strip[:, 32 - (K - 1): 32 - (K - 1) + 512],
                                             fwsb[:, base:base + 1], None, ALU.mult))
                        for j in range(1, K):
                            pr.gatec(e)
                            pr.t(e.scalar_tensor_tensor(acc[:], strip[:, 32 - (K - 1) + j: 32 - (K - 1) + j + 512],
                                                        fwsb[:, base + j:base + j + 1],
                                                        acc[:], ALU.mult, ALU.add))
                    pr.on('vector', fcv)

                    def fcb(e):
                        pr.gatec(e)
                        pr.t(e.copy(fbf[:], acc[:]))
                    pr.on('scalar', fcb)
                    for q in range(4):
                        def ftr(e, q=q):
                            pr.gate(e)
                            pr.t(e.transpose(pT[:, 0:128], fbf[:, q * 128:(q + 1) * 128], sid[:]))
                        pr.on('tensor', ftr)

                        def fcp(e, q=q, s=s, cb=cb):
                            pr.gatec(e)
                            pr.t(e.copy(firt[:, s * 4096 + q * 1024 + cb * 128: s * 4096 + q * 1024 + (cb + 1) * 128],
                                        pT[:, 0:128]))
                        pr.on('scalar', fcp)

            for r4 in range(4):
                def fst_(e, r4=r4):
                    pr.gate(e)
                    for s in range(7):
                        for h in range(4):
                            if s < 5:
                                src = firt[:, s * 4096 + r4 * 1024 + h * 256: s * 4096 + r4 * 1024 + (h + 1) * 256]
                            elif s == 5:
                                src = delt[:, r4 * 1024 + h * 256: r4 * 1024 + (h + 1) * 256]
                            else:
                                src = v4t[:, r4 * 1024 + h * 256: r4 * 1024 + (h + 1) * 256]
                            pr.t(e.reduce_sum(stats[:, r4 * 28 + s * 4 + h: r4 * 28 + s * 4 + h + 1],
                                              src, AX.X, apply_absolute_value=True))
                    pr.gatec(e)
                    pr.t(e.tensor_scalar_mul(stats[:, r4 * 28:(r4 + 1) * 28],
                                             stats[:, r4 * 28:(r4 + 1) * 28], 1.0 / 256))
                pr.on('vector', fst_)

                def fsb(e, r4=r4):
                    pr.gatec(e)
                    pr.t(e.copy(statb[:], stats[:, r4 * 28:(r4 + 1) * 28]))
                pr.on('scalar', fsb)

                def ftr(e, r4=r4):
                    pr.gate(e)
                    pr.t(e.transpose(pT[0:28, 0:128], statb[:], sid[:]))
                pr.on('tensor', ftr)

                def fcp(e, r4=r4):
                    pr.gatec(e)
                    pr.t(e.copy(gif[0:28, 8 * 512 + r4 * 128: 8 * 512 + (r4 + 1) * 128], pT[0:28, 0:128]))
                pr.on('scalar', fcp)

            for jo in range(16):
                def fw1(e, jo=jo):
                    pr.gatec(e)
                    pr.dma(e, gw1s[:].rearrange("p (kc j) -> p kc j", j=128),
                           gw1[0:1024, jo * 128:(jo + 1) * 128].rearrange("(kc p) j -> p kc j", p=128))
                    pr.dma(e, gw1c[:], gw1[1024:1052, jo * 128:(jo + 1) * 128])
                pr.on('sync', fw1)

                def fmm(e, jo=jo):
                    pr.gate(e)
                    for ki in range(8):
                        e.matmul(pA[:, 0:512], gw1s[:, ki * 128:(ki + 1) * 128],
                                 gif[:, ki * 512:(ki + 1) * 512], start=(ki == 0), stop=False)
                    pr.t(e.matmul(pA[:, 0:512], gw1c[:], gif[0:28, 8 * 512:9 * 512],
                                  start=False, stop=True))
                pr.on('tensor', fmm)

                def fev(e, jo=jo):
                    pr.gatec(e)
                    pr.t(e.activation(hmid[:, jo * 512:(jo + 1) * 512], pA[:, 0:512],
                                      AF.Gelu, bias=gb1s[:, jo:jo + 1]))
                pr.on('scalar', fev)

            for r4 in range(4):
                def fmm2(e, r4=r4):
                    pr.gatec(e)
                    for ki in range(16):
                        m = e.matmul(pL[:, 0:28], hmid[:, ki * 512 + r4 * 128: ki * 512 + (r4 + 1) * 128],
                                     gw2s[:, ki * 28:(ki + 1) * 28], start=(ki == 0), stop=(ki == 15))
                        if ki == 15:
                            pr.t(m)
                pr.on('tensor', fmm2)

                def fsp(e, r4=r4):
                    pr.gatec(e)
                    pr.t(e.tensor_add(spt[:, 0:28], pL[:, 0:28], b2bc[:]))
                pr.on('vector', fsp)

                def fsp2(e):
                    pr.gatec(e)
                    pr.t(e.activation(spt[:, 28:56], spt[:, 0:28], AF.Abs))
                    pr.gatec(e)
                    pr.t(e.activation(spt[:, 28:56], spt[:, 28:56], AF.Exp, scale=-1.0))
                    pr.gatec(e)
                    pr.t(e.activation(spt[:, 28:56], spt[:, 28:56], AF.Ln, bias=1.0))
                pr.on('scalar', fsp2)

                def fsp3(e, r4=r4):
                    pr.gatec(e)
                    pr.t(e.tensor_scalar_max(spt[:, 56:84], spt[:, 0:28], 0.0))
                    pr.gatec(e)
                    pr.t(e.tensor_add(spt[:, 56:84], spt[:, 28:56], spt[:, 56:84]))
                    pr.gatec(e)
                    pr.t(e.tensor_scalar_add(spt[:, 56:84], spt[:, 56:84], 0.02))
                    for h in range(4):
                        pr.gatec(e)
                        pr.t(e.reduce_sum(cols[:, 8 + h:9 + h], spt[:, 56 + h * 7:56 + (h + 1) * 7], AX.X))
                        pr.gatec(e)
                        pr.t(e.reciprocal(cols[:, 12:13], cols[:, 8 + h:9 + h]))
                        pr.gatec(e)
                        pr.t(e.tensor_scalar(wts[:, r4 * 28 + h * 7: r4 * 28 + (h + 1) * 7],
                                             spt[:, 56 + h * 7:56 + (h + 1) * 7],
                                             cols[:, 12:13], None, ALU.mult))
                pr.on('vector', fsp3)

            for r4 in range(4):
                def fcmb(e, r4=r4):
                    pr.gatec(e)
                    for h in range(4):
                        pr.t(e.tensor_scalar(ot[:, h * 256:(h + 1) * 256],
                                             firt[:, 0 + r4 * 1024 + h * 256: 0 + r4 * 1024 + (h + 1) * 256],
                                             wts[:, r4 * 28 + h * 7: r4 * 28 + h * 7 + 1], None, ALU.mult))
                    for s in range(1, 7):
                        pr.gatec(e)
                        for h in range(4):
                            if s < 5:
                                src = firt[:, s * 4096 + r4 * 1024 + h * 256: s * 4096 + r4 * 1024 + (h + 1) * 256]
                            elif s == 5:
                                src = delt[:, r4 * 1024 + h * 256: r4 * 1024 + (h + 1) * 256]
                            else:
                                src = v4t[:, r4 * 1024 + h * 256: r4 * 1024 + (h + 1) * 256]
                            pr.t(e.scalar_tensor_tensor(ot[:, h * 256:(h + 1) * 256], src,
                                                        wts[:, r4 * 28 + h * 7 + s: r4 * 28 + h * 7 + s + 1],
                                                        ot[:, h * 256:(h + 1) * 256], ALU.mult, ALU.add))
                pr.on('vector', fcmb)

                def fsq(e):
                    pr.gatec(e)
                    pr.t(e.activation(sq[:], ot[:], AF.Square))
                pr.on('scalar', fsq)

                def fms(e):
                    pr.gatec(e)
                    for h in range(4):
                        pr.t(e.reduce_sum(cols[:, h:h + 1], sq[:, h * 256:(h + 1) * 256], AX.X))
                    pr.gatec(e)
                    pr.t(e.tensor_scalar_mul(cols[:, 0:4], cols[:, 0:4], 1.0 / 256))
                pr.on('vector', fms)

                def frs(e):
                    pr.gatec(e)
                    pr.t(e.activation(cols[:, 4:8], cols[:, 0:4], AF.Ln, bias=kq[1e-5]))
                    pr.gatec(e)
                    pr.t(e.activation(cols[:, 4:8], cols[:, 4:8], AF.Exp, scale=-0.5))
                pr.on('scalar', frs)

                def fnw(e, r4=r4):
                    for h in range(4):
                        pr.gatec(e)
                        pr.t(e.tensor_scalar(ot[:, h * 256:(h + 1) * 256], ot[:, h * 256:(h + 1) * 256],
                                             cols[:, 4 + h:5 + h], None, ALU.mult))
                    pr.gatec(e)
                    pr.t(e.tensor_mul(ob[:], ot[:], nwb[:]))
                pr.on('vector', fnw)
                for kc2 in range(8):
                    def ftr(e, kc2=kc2):
                        pr.gate(e)
                        pr.t(e.transpose(pT[:, 0:128], ob[:, kc2 * 128:(kc2 + 1) * 128], sid[:]))
                    pr.on('tensor', ftr)

                    def fcp(e, kc2=kc2, r4=r4):
                        pr.gatec(e)
                        pr.t(e.copy(off_[:, kc2 * 512 + r4 * 128: kc2 * 512 + (r4 + 1) * 128], pT[:, 0:128]))
                    pr.on('scalar', fcp)

            for do in range(8):
                def fwo(e, do=do):
                    pr.gatec(e)
                    pr.dma(e, wosb[:].rearrange("p (kc j) -> p kc j", j=128),
                           wo[:, do * 128:(do + 1) * 128].rearrange("(kc p) j -> p kc j", p=128))
                pr.on('sync', fwo)

                def fmm3(e, do=do):
                    pr.gate(e)
                    for ki in range(8):
                        m = e.matmul(pA[:, 0:512], wosb[:, ki * 128:(ki + 1) * 128],
                                     off_[:, ki * 512:(ki + 1) * 512], start=(ki == 0), stop=(ki == 7))
                        if ki == 7:
                            pr.t(m)
                pr.on('tensor', fmm3)

                def fev2(e, do=do):
                    pr.gatec(e)
                    pr.t(e.copy(tmp[:], pA[:, 0:512]))
                pr.on('scalar', fev2)
                for q in range(4):
                    def ftr(e, q=q):
                        pr.gate(e)
                        pr.t(e.transpose(pTf[:, 0:128], tmp[:, q * 128:(q + 1) * 128], sidf[:]))
                    pr.on('tensor', ftr)

                    def fadd(e, do=do, q=q):
                        pr.gatec(e)
                        pr.gates(e)
                        pr.t(e.tensor_add(otok4[:, q * 1024 + do * 128: q * 1024 + (do + 1) * 128],
                                          pTf[:, 0:128], xt4[:, q * 1024 + do * 128: q * 1024 + (do + 1) * 128]))
                    pr.on('vector', fadd)

            for q in range(4):
                def fsto(e, q=q, blk=blk):
                    pr.gatec(e)
                    pr.sto(e, out[blk * 512 + q * 128: blk * 512 + (q + 1) * 128, :],
                           otok4[:, q * 1024:(q + 1) * 1024])
                pr.on('gpsimd', fsto)

        def fin(e):
            e.wait_ge(st, pr.s)
        pr.on('gpsimd', fin)
    stack.close()
    return nc



def build_b2():
    nc, stack, ld, cs, st = mkprog("b2")
    SQ = SEQ
    NCH = SQ // 32
    qf = nc.dram_tensor("qf", [2, 256, SQ], BF16, kind="ExternalInput").ap()
    kf = nc.dram_tensor("kf", [2, 256, SQ], BF16, kind="ExternalInput").ap()
    qt = nc.dram_tensor("qt", [2, SQ, 256], BF16, kind="ExternalInput").ap()
    kt = nc.dram_tensor("kt", [2, SQ, 256], BF16, kind="ExternalInput").ap()
    vt = nc.dram_tensor("vt", [2, SQ, 256], BF16, kind="ExternalInput").ap()
    bt = nc.dram_tensor("bt", [2, SQ, 1], F32, kind="ExternalInput").ap()
    msl = nc.dram_tensor("c_msl", [32, 32], F32, kind="ExternalInput").ap()
    msu = nc.dram_tensor("c_msu", [32, 32], F32, kind="ExternalInput").ap()
    mud = nc.dram_tensor("c_mud", [32, 32], F32, kind="ExternalInput").ap()
    id32 = nc.dram_tensor("c_id32", [32, 32], F32, kind="ExternalInput").ap()
    idf = nc.dram_tensor("c_idf", [128, 128], F32, kind="ExternalInput").ap()
    idb = nc.dram_tensor("c_idb", [128, 128], BF16, kind="ExternalInput").ap()
    onesf = nc.dram_tensor("c_ones", [1, 128], F32, kind="ExternalInput").ap()
    dout = nc.dram_tensor("dout", [2, SQ, 256], F32, kind="ExternalOutput").ap()

    sbt = lambda n, shp, dt_: stack.enter_context(nc.sbuf_tensor(n, shp, dt_))
    ps = lambda n, shp: stack.enter_context(nc.psum_tensor(n, shp, F32))
    with nc.Block() as block:
        pr = P(nc, block, ld, cs, st)
        kq = mkconsts(pr, stack, nc, [1e-6])
        smsl = sbt("smsl", [32, 32], F32)
        smsu = sbt("smsu", [32, 32], F32)
        smud = sbt("smud", [32, 32], F32)
        sid32 = sbt("sid32", [32, 32], F32)
        s2i = sbt("s2i", [32, 32], F32)
        sidf = sbt("sidf", [128, 128], F32)
        sid = sbt("sid", [128, 128], BF16)
        sones = sbt("sones", [1, 128], F32)
        qtc = sbt("qtc", [32, 256], BF16)
        ktc = sbt("ktc", [32, 256], BF16)
        vtc = sbt("vtc", [32, 256], BF16)
        qfs = sbt("qfs", [128, 64], BF16)
        kfs = sbt("kfs", [128, 64], BF16)
        betac = sbt("betac", [32, 1], F32)
        sq2 = sbt("sq2", [32, 256], F32)
        ncol = sbt("ncol", [32, 8], F32)
        rowb = sbt("rowb", [1, 32], F32)
        mbc1 = sbt("mbc1", [32, 32], F32)
        mbc2 = sbt("mbc2", [32, 32], F32)
        Nsb = sbt("Nsb", [32, 32], F32)
        Mt = sbt("Mt", [32, 32], F32)
        Xs = sbt("Xs", [32, 32], F32)
        Zs = sbt("Zs", [32, 32], F32)
        Xt_sb = sbt("Xt_sb", [32, 32], F32)
        Xt_bf = sbt("Xt_bf", [32, 32], BF16)
        attnT = sbt("attnT", [32, 32], BF16)
        rhs_sb = sbt("rhs_sb", [32, 512], BF16)
        u_f = sbt("u_f", [32, 256], F32)
        w_bf = sbt("w_bf", [32, 256], BF16)
        wT = sbt("wT", [128, 64], BF16)
        uadj_b = sbt("uadj_b", [32, 256], BF16)
        kn_b = sbt("kn_b", [32, 256], BF16)
        osb = sbt("osb", [32, 256], F32)
        S_f0 = sbt("S_f0", [128, 256], F32)
        S_f1 = sbt("S_f1", [128, 256], F32)
        S_b0 = sbt("S_b0", [128, 256], BF16)
        S_b1 = sbt("S_b1", [128, 256], BF16)
        pKK = ps("pKK", [32, 32])
        pKQ = ps("pKQ", [32, 32])
        pNw = ps("pNw", [32, 32])
        pBc = ps("pBc", [32, 32])
        pBig = ps("pBig", [32, 512])
        pS0 = ps("pS0", [128, 256])
        pS1 = ps("pS1", [128, 256])
        pTb = stack.enter_context(nc.psum_tensor("pTb", [128, 64], BF16))

        def f0(e):
            pr.dma(e, smsl[:], msl)
            pr.dma(e, smsu[:], msu)
            pr.dma(e, smud[:], mud)
            pr.dma(e, sid32[:], id32)
            pr.dma(e, sidf[:], idf)
            pr.dma(e, sid[:], idb)
            pr.dma(e, sones[:], onesf)
        pr.on('sync', f0)

        def f0b(e):
            pr.gate(e)
            pr.t(e.tensor_scalar_mul(s2i[:], sid32[:], 2.0))
        pr.on('vector', f0b)

        for pair in range(2):
            def fz(e, pair=pair):
                pr.gatec(e)
                pr.t(e.memset(S_f0[:], 0.0))
                pr.t(e.memset(S_f1[:], 0.0))
                pr.t(e.memset(S_b0[:], 0.0))
                pr.t(e.memset(S_b1[:], 0.0))
            pr.on('vector', fz)

            for c in range(NCH):
                def fl(e, pair=pair, c=c):
                    pr.gatec(e)
                    pr.gates(e)
                    pr.dma(e, qtc[:], qt[pair][c * 32:(c + 1) * 32, :])
                    pr.dma(e, ktc[:], kt[pair][c * 32:(c + 1) * 32, :])
                    pr.dma(e, vtc[:], vt[pair][c * 32:(c + 1) * 32, :])
                    pr.dma(e, qfs[:, 0:32], qf[pair][0:128, c * 32:(c + 1) * 32])
                    pr.dma(e, qfs[:, 32:64], qf[pair][128:256, c * 32:(c + 1) * 32])
                    pr.dma(e, kfs[:, 0:32], kf[pair][0:128, c * 32:(c + 1) * 32])
                    pr.dma(e, kfs[:, 32:64], kf[pair][128:256, c * 32:(c + 1) * 32])
                    pr.dma(e, betac[:], bt[pair][c * 32:(c + 1) * 32, :])
                pr.on('sync', fl)

                def n1(e):
                    pr.gate(e)
                    pr.t(e.activation(sq2[:], qtc[:], AF.Square))
                pr.on('scalar', n1)

                def n2(e):
                    pr.gatec(e)
                    pr.t(e.reduce_sum(ncol[:, 0:1], sq2[:], AX.X))
                pr.on('vector', n2)

                def n3(e):
                    pr.gatec(e)
                    pr.t(e.activation(sq2[:], ktc[:], AF.Square))
                pr.on('scalar', n3)

                def n4(e):
                    pr.gatec(e)
                    pr.t(e.reduce_sum(ncol[:, 1:2], sq2[:], AX.X))
                pr.on('vector', n4)

                def n5(e):
                    pr.gatec(e)
                    pr.t(e.activation(ncol[:, 2:3], ncol[:, 0:1], AF.Ln, bias=kq[1e-6][0:32, :]))
                    pr.gatec(e)
                    pr.t(e.activation(ncol[:, 2:3], ncol[:, 2:3], AF.Exp, scale=-0.5))
                    pr.gatec(e)
                    pr.t(e.activation(ncol[:, 3:4], ncol[:, 1:2], AF.Ln, bias=kq[1e-6][0:32, :]))
                    pr.gatec(e)
                    pr.t(e.activation(ncol[:, 3:4], ncol[:, 3:4], AF.Exp, scale=-0.5))
                pr.on('scalar', n5)

                def n6(e):
                    pr.gatec(e)
                    pr.t(e.tensor_mul(ncol[:, 4:5], betac[:], ncol[:, 3:4]))
                pr.on('vector', n6)

                for (ci, mask, dst) in ((3, smsl, mbc1), (4, smsu, mbc2)):
                    def b1(e, ci=ci):
                        pr.gatec(e)
                        pr.t(e.transpose(pNw[0:1, 0:32], ncol[:, ci:ci + 1], sidf[0:32, 0:32]))
                    pr.on('tensor', b1)

                    def b2(e):
                        pr.gatec(e)
                        pr.t(e.copy(rowb[:], pNw[0:1, 0:32]))
                    pr.on('scalar', b2)

                    def b3(e):
                        pr.gatec(e)
                        pr.t(e.matmul(pBc[:, 0:32], sones[0:1, 0:32], rowb[:], start=True, stop=True))
                    pr.on('tensor', b3)

                    def b4(e, mask=mask, dst=dst):
                        pr.gatec(e)
                        pr.t(e.tensor_mul(dst[:], pBc[:, 0:32], mask[:]))
                    pr.on('vector', b4)

                def g1(e):
                    pr.gatec(e)
                    e.matmul(pKK[:], kfs[:, 0:32], kfs[:, 0:32], start=True, stop=False)
                    pr.t(e.matmul(pKK[:], kfs[:, 32:64], kfs[:, 32:64], start=False, stop=True))
                    e.matmul(pKQ[:], kfs[:, 0:32], qfs[:, 0:32], start=True, stop=False)
                    pr.t(e.matmul(pKQ[:], kfs[:, 32:64], qfs[:, 32:64], start=False, stop=True))
                pr.on('tensor', g1)

                def g2(e):
                    pr.gatec(e)
                    pr.t(e.scalar_tensor_tensor(Nsb[:], pKK[:], ncol[:, 4:5], mbc1[:], ALU.mult, ALU.mult))
                    pr.gatec(e)
                    pr.t(e.scalar_tensor_tensor(Mt[:], pKK[:], ncol[:, 3:4], mbc2[:], ALU.mult, ALU.mult))
                    pr.gatec(e)
                    pr.t(e.scalar_tensor_tensor(attnT[:], pKQ[:], ncol[:, 3:4], smud[:], ALU.mult, ALU.mult))
                pr.on('vector', g2)

                def g3(e):
                    pr.gatec(e)
                    pr.t(e.scalar_tensor_tensor(Xs[:], Nsb[:], -1.0, sid32[:], ALU.mult, ALU.add))
                    pr.gatec(e)
                    pr.t(e.tensor_add(Mt[:], Mt[:], sid32[:]))
                pr.on('vector', g3)

                for it in range(4):
                    def w1(e):
                        pr.gatec(e)
                        pr.t(e.matmul(pNw[:], Mt[:], Xs[:], start=True, stop=True))
                    pr.on('tensor', w1)

                    def w2(e):
                        pr.gatec(e)
                        pr.t(e.scalar_tensor_tensor(Zs[:], pNw[:], -1.0, s2i[:], ALU.mult, ALU.add))
                    pr.on('vector', w2)

                    def w3(e):
                        pr.gatec(e)
                        pr.t(e.transpose(pNw[:], Xs[:], sidf[0:32, 0:32]))
                    pr.on('tensor', w3)

                    def w4(e):
                        pr.gatec(e)
                        pr.t(e.copy(Xt_sb[:], pNw[:]))
                    pr.on('scalar', w4)

                    def w5(e):
                        pr.gatec(e)
                        pr.t(e.matmul(pNw[:], Xt_sb[:], Zs[:], start=True, stop=True))
                    pr.on('tensor', w5)

                    def w6(e):
                        pr.gatec(e)
                        pr.t(e.copy(Xs[:], pNw[:]))
                    pr.on('scalar', w6)

                def a1(e):
                    pr.gatec(e)
                    pr.t(e.transpose(pNw[:], Xs[:], sidf[0:32, 0:32]))
                pr.on('tensor', a1)

                def a2(e):
                    pr.gatec(e)
                    pr.t(e.copy(Xt_bf[:], pNw[:]))
                pr.on('scalar', a2)

                def a3(e):
                    pr.gatec(e)
                    pr.t(e.tensor_scalar(rhs_sb[:, 0:256], vtc[:], betac[:, 0:1], None, ALU.mult))
                    pr.gatec(e)
                    pr.t(e.tensor_scalar(rhs_sb[:, 256:512], ktc[:], ncol[:, 4:5], None, ALU.mult))
                    pr.gatec(e)
                    pr.t(e.tensor_scalar(kn_b[:], ktc[:], ncol[:, 3:4], None, ALU.mult))
                pr.on('vector', a3)

                def a4(e):
                    pr.gatec(e)
                    pr.t(e.matmul(pBig[:, 0:512], Xt_bf[:], rhs_sb[:], start=True, stop=True))
                pr.on('tensor', a4)

                def a5(e):
                    pr.gatec(e)
                    pr.t(e.copy(u_f[:], pBig[:, 0:256]))
                    pr.gatec(e)
                    pr.t(e.copy(w_bf[:], pBig[:, 256:512]))
                pr.on('scalar', a5)

                def a6(e):
                    pr.gatec(e)
                    pr.t(e.transpose(pTb[:, 0:32], w_bf[:, 0:128], sid[0:32, 0:32]))
                    pr.gatec(e)
                    pr.t(e.transpose(pTb[:, 32:64], w_bf[:, 128:256], sid[0:32, 0:32]))
                pr.on('tensor', a6)

                def a7(e):
                    pr.gatec(e)
                    pr.t(e.copy(wT[:], pTb[:]))
                pr.on('scalar', a7)

                def a8(e):
                    pr.gatec(e)
                    e.matmul(pBig[:, 0:256], wT[:, 0:32], S_b0[:], start=True, stop=False)
                    pr.t(e.matmul(pBig[:, 0:256], wT[:, 32:64], S_b1[:], start=False, stop=True))
                pr.on('tensor', a8)

                def a9(e):
                    pr.gatec(e)
                    pr.t(e.tensor_sub(u_f[:], u_f[:], pBig[:, 0:256]))
                pr.on('vector', a9)

                def a10(e):
                    pr.gatec(e)
                    pr.t(e.copy(uadj_b[:], u_f[:]))
                pr.on('scalar', a10)

                def a11(e):
                    pr.gatec(e)
                    e.matmul(pBig[:, 0:256], qfs[:, 0:32], S_b0[:], start=True, stop=False)
                    e.matmul(pBig[:, 0:256], qfs[:, 32:64], S_b1[:], start=False, stop=False)
                    pr.t(e.matmul(pBig[:, 0:256], attnT[:], uadj_b[:], start=False, stop=True))
                pr.on('tensor', a11)

                def a12(e):
                    pr.gatec(e)
                    pr.t(e.tensor_scalar(osb[:], pBig[:, 0:256], ncol[:, 2:3], None, ALU.mult))
                pr.on('vector', a12)

                def a13(e, pair=pair, c=c):
                    pr.gatec(e)
                    pr.sto(e, dout[pair][c * 32:(c + 1) * 32, :], osb[:])
                pr.on('gpsimd', a13)

                def a14(e):
                    pr.gatec(e)
                    e.matmul(pS0[:], kn_b[:, 0:128], uadj_b[:], start=True, stop=True)
                    pr.t(e.matmul(pS1[:], kn_b[:, 128:256], uadj_b[:], start=True, stop=True))
                pr.on('tensor', a14)

                def a15(e):
                    pr.gatec(e)
                    pr.t(e.tensor_add(S_f0[:], S_f0[:], pS0[:]))
                    pr.gatec(e)
                    pr.t(e.tensor_add(S_f1[:], S_f1[:], pS1[:]))
                pr.on('vector', a15)

                def a16(e):
                    pr.gatec(e)
                    pr.t(e.copy(S_b0[:], S_f0[:]))
                    pr.gatec(e)
                    pr.t(e.copy(S_b1[:], S_f1[:]))
                pr.on('scalar', a16)

        def fin(e):
            e.wait_ge(st, pr.s)
        pr.on('gpsimd', fin)
    stack.close()
    return nc


_b1_nc = _b2_nc = _b3_nc = None


def run_deltanet(x4_full, wb, raw):
    global _b1_nc, _b2_nc, _b3_nc
    if _b1_nc is None:
        _b1_nc = build_b1()
    if _b2_nc is None:
        _b2_nc = build_b2()
    if _b3_nc is None:
        _b3_nc = build_b3()
    idb = np.eye(128, dtype=BF)
    idf = np.eye(128, dtype=np.float32)
    ones = np.ones((1, 128), dtype=np.float32)
    ctile = lambda w: np.ascontiguousarray(w.reshape(8, 128, w.shape[1]).transpose(1, 0, 2).reshape(128, -1))
    cq = ctile(raw['conv_q_w'])
    ck = ctile(raw['conv_k_w'])
    cv = ctile(raw['conv_v_w'])
    im = []
    for c in range(8):
        t0 = c * TPC
        xa = np.zeros((TA, D), np.float32)
        xa[128:] = x4_full[t0:t0 + TPC]
        if c % 2 == 1:
            xa[:128] = x4_full[t0 - 128:t0]
        im.append({"x": xa, "wq": wb['l1_wq'], "wk": wb['l1_wk'], "wv": wb['l1_wv'],
                   "wbv": wb['l1_wb'], "cq": cq, "ck": ck, "cv": cv, "c_idb": idb})
    r1 = run_bass_kernel_spmd(_b1_nc, im, core_ids=list(range(8)))
    r1 = r1.results

    qf_b, kf_b, qt_b, kt_b, vt_b, bt_b = [], [], [], [], [], []
    for b in range(4):
        e_, o_ = r1[2 * b], r1[2 * b + 1]
        qf_b.append(np.concatenate([e_["qf"][:, 128:], o_["qf"][:, 128:]], axis=1))
        kf_b.append(np.concatenate([e_["kf"][:, 128:], o_["kf"][:, 128:]], axis=1))
        qt_b.append(np.concatenate([e_["qt"][128:], o_["qt"][128:]], axis=0))
        kt_b.append(np.concatenate([e_["kt"][128:], o_["kt"][128:]], axis=0))
        vt_b.append(np.concatenate([e_["vt"][128:], o_["vt"][128:]], axis=0))
        bt_b.append(np.concatenate([e_["bt"][128:], o_["bt"][128:]], axis=0))

    msl = np.tril(np.ones((32, 32), np.float32), -1)
    msu = np.triu(np.ones((32, 32), np.float32), 1)
    mud = np.triu(np.ones((32, 32), np.float32))
    id32 = np.eye(32, dtype=np.float32)
    im2 = []
    for c in range(8):
        prs = [2 * c, 2 * c + 1]
        dd = {"c_msl": msl, "c_msu": msu, "c_mud": mud, "c_id32": id32,
              "c_idf": idf, "c_idb": idb, "c_ones": ones}
        for nm, srcb, ax in (("qf", qf_b, 0), ("kf", kf_b, 0)):
            dd[nm] = np.stack([np.ascontiguousarray(srcb[p // 4][(p % 4) * 256:(p % 4 + 1) * 256]) for p in prs])
        for nm, srcb in (("qt", qt_b), ("kt", kt_b), ("vt", vt_b)):
            dd[nm] = np.stack([np.ascontiguousarray(srcb[p // 4][:, (p % 4) * 256:(p % 4 + 1) * 256]) for p in prs])
        dd["bt"] = np.stack([np.ascontiguousarray(bt_b[p // 4][:, p % 4:p % 4 + 1]) for p in prs])
        im2.append(dd)
    r2 = run_bass_kernel_spmd(_b2_nc, im2, core_ids=list(range(8)))
    r2 = r2.results
    delta_full = np.zeros((4, SEQ, 4, 256), np.float32)
    for c in range(8):
        for i, p in enumerate((2 * c, 2 * c + 1)):
            delta_full[p // 4, :, p % 4, :] = r2[c]["dout"][i]
    delta_full = delta_full.reshape(TT, 1024)

    firw = np.zeros((1024, 57), np.float32)
    col = 0
    for nm, K in (("fir1", 1), ("fir3", 3), ("fir7", 7), ("fir15", 15), ("fir31", 31)):
        firw[:, col:col + K] = raw[nm]
        col += K
    fw = np.ascontiguousarray(firw.reshape(8, 128, 57).transpose(1, 0, 2).reshape(128, 8 * 57))
    gb1 = np.ascontiguousarray(raw['gate_b1'].reshape(16, 128).T)
    gb2 = raw['gate_b2'].reshape(1, 28).astype(np.float32)
    nwt = np.tile(raw['norm_w'], 4).reshape(1, 1024).astype(np.float32)
    im3 = []
    for c in range(8):
        t0 = c * TPC
        im3.append({"x": np.ascontiguousarray(x4_full[t0:t0 + TPC]),
                    "vf": r1[c]["vf"], "vt": r1[c]["vt"],
                    "dl": np.ascontiguousarray(delta_full[t0:t0 + TPC]),
                    "fw": fw, "gw1": wb['l1_gw1'], "gb1": gb1, "gw2": wb['l1_gw2'],
                    "gb2": gb2, "nw": nwt, "wo": wb['l1_wo'],
                    "c_idb": idb, "c_idf": idf, "c_ones": ones})
    r3 = run_bass_kernel_spmd(_b3_nc, im3, core_ids=list(range(8)))
    return np.concatenate([r["out"] for r in r3.results], axis=0)



def kernel(**inp):
    a = lambda n: np.ascontiguousarray(np.asarray(inp[n]))
    wf = {
        'l0_wgT': np.ascontiguousarray(a('L0_ffn_wg').T),
        'l0_wgateT': np.ascontiguousarray(np.transpose(a('L0_ffn_Wgate'), (0, 2, 1))),
        'l0_wupT': np.ascontiguousarray(np.transpose(a('L0_ffn_Wup'), (0, 2, 1))),
        'l0_wdownT': np.ascontiguousarray(np.transpose(a('L0_ffn_Wdown'), (0, 2, 1))),
        'l0_winT': np.ascontiguousarray(a('L0_attn_W_in').T),
        'l0_woutT': np.ascontiguousarray(a('L0_attn_W_out').T),
        'l1_w1T': np.ascontiguousarray(a('L1_ffn_w1_w').T),
        'l1_w2T': np.ascontiguousarray(a('L1_ffn_w2_w').T),
        'l1_h1T': np.ascontiguousarray(a('L1_ffn_h1_w').T),
        'l1_h2T': np.ascontiguousarray(a('L1_ffn_h2_w').T),
        'l1_w3T': np.ascontiguousarray(a('L1_ffn_w3_w').T),
        'l1_wq': a('L1_attn_w_q'), 'l1_wk': a('L1_attn_w_k'), 'l1_wv': a('L1_attn_w_v'),
        'l1_wb': a('L1_attn_w_b'), 'l1_gw1': a('L1_attn_gate_w1'),
        'l1_gw2': a('L1_attn_gate_w2'), 'l1_wo': a('L1_attn_w_o'),
    }
    wb = wcast_all(wf)

    sc = lambda v: np.ascontiguousarray(np.asarray(v).reshape(-1, 128).T)
    tsc = {'w1_s': sc(inp['L1_ffn_w1_s']), 'w2_s': sc(inp['L1_ffn_w2_s']),
           'h1_s': sc(inp['L1_ffn_h1_s']), 'h2_s': sc(inp['L1_ffn_h2_s']),
           'w3_s': sc(inp['L1_ffn_w3_s'])}
    raw_m = {'conv_w': a('L0_attn_conv_w'), 'conv_b': a('L0_attn_conv_b'),
             'dt_bias': a('L0_attn_dt_bias'), 'A_log': a('L0_attn_A_log'),
             'D': a('L0_attn_D'), 'norm_w': a('L0_attn_norm_w')}
    raw_d = {'conv_q_w': a('L1_attn_conv_q_w'), 'conv_k_w': a('L1_attn_conv_k_w'),
             'conv_v_w': a('L1_attn_conv_v_w'), 'fir1': a('L1_attn_fir1'),
             'fir3': a('L1_attn_fir3'), 'fir7': a('L1_attn_fir7'),
             'fir15': a('L1_attn_fir15'), 'fir31': a('L1_attn_fir31'),
             'gate_b1': a('L1_attn_gate_b1'), 'gate_b2': a('L1_attn_gate_b2'),
             'norm_w': a('L1_attn_norm_w')}
    bg = a('L0_ffn_bg')

    x = np.ascontiguousarray(np.asarray(inp['L0_attn_x']).reshape(TT, D))
    shard = lambda arr: [np.ascontiguousarray(arr[c * TPC:(c + 1) * TPC]) for c in range(8)]

    x1 = np.concatenate(run_moe(shard(x), wb, bg), axis=0)
    x2 = run_mamba(x1, wb, raw_m)
    x3 = np.concatenate(run_moe(shard(x2), wb, bg), axis=0)
    x4 = np.concatenate(run_tffn(shard(x3), wb, tsc), axis=0)
    x5 = run_deltanet(x4, wb, raw_d)
    out = np.concatenate(run_tffn(shard(x5), wb, tsc), axis=0)
    return out.reshape(BSZ, SEQ, D).astype(np.float32)
